# version 01
# baseline (speedup 1.0000x reference)
import numpy as np
import concourse.bass as bass
import concourse.mybir as mybir
from concourse.bass_utils import run_bass_kernel_spmd
from concourse.bass_core_rust import IndirectOffsetOnAxis
from contextlib import ExitStack

F32 = mybir.dt.float32
BF16 = mybir.dt.bfloat16
U32 = mybir.dt.uint32
U8 = mybir.dt.uint8
AF = mybir.ActivationFunctionType
ALU = mybir.AluOpType
AX = mybir.AxisListType

B, T, D = 4, 8192, 1024
NT = 64
NS = 32
KROW = 1056
NEG30 = -1.0e30
NEG9 = -1.0e9
SC32 = float(np.float32(1.0) / np.sqrt(np.float32(32.0)))
SCH = 0.125
LOGK_Q = float(np.log(np.float32(1152.0)))
LOGK_KV = float(np.log(np.float32(2304.0)))
PADQ, PADKV = 24, 40
NBIS = 42
ENG_NAMES = ["sync", "scalar", "vector", "tensor", "gpsimd"]


class Prog:

    def __init__(self, nc, sems, cnt):
        self.nc = nc
        self.sems = sems
        self.cnt = cnt
        self.ops = {e: [] for e in ENG_NAMES}

    def op(self, engine, fn, waits=(), incs=()):
        wvals = [(self.sems[n], self.cnt[n] if v is None else v) for n, v in waits]
        incs = list(incs)
        for n, k in incs:
            self.cnt[n] += k

        def emit(h, wvals=wvals, fn=fn, incs=incs):
            for s, val in wvals:
                if val > 0:
                    h.wait_ge(s, val)
            ins = fn(h)
            for n, k in incs:
                ins.then_inc(self.sems[n], k)
        self.ops[engine].append(emit)

    def flush(self, name):
        ops = self.ops
        self.ops = {e: [] for e in ENG_NAMES}
        with self.nc.Block(name) as blk:
            for e in ENG_NAMES:
                if ops[e]:
                    getattr(blk, e)(lambda h, fns=ops[e]: [f(h) for f in fns])


def wslot(s):
    return 512 * ((256 * s + 256 + 511) // 512)


def build(debug=0):
    nc = bass.Bass("TRN2", target_bir_lowering=False, debug=False)

    def inp(name, shape, dt):
        return nc.dram_tensor(name, shape, dt, kind="ExternalInput").ap()

    x_all = inp("x_all", [T, D], F32)
    x_own = inp("x_own", [NS * 128, D], F32)
    rowid_f = inp("rowid_f", [128, NS], F32)
    bound_f = inp("bound_f", [128, NS], F32)
    rowid_u = inp("rowid_u", [128, NS], U32)
    ownmap = inp("ownmap", [T, 1], U32)
    wqT = inp("wqT", [D, 32], F32)
    wkT = inp("wkT", [D, 32], F32)
    gain_in = inp("gain_in", [1, D], F32)
    bias_in = inp("bias_in", [1, D], F32)
    log_mix = inp("log_mix", [1, 1], F32)
    log_scale = inp("log_scale", [1, 1], F32)
    gam_l = inp("gam_l", [1, D], F32)
    gam_h = inp("gam_h", [1, D], F32)
    wq_l = inp("wq_l", [D, 512], F32)
    wkv_l = inp("wkv_l", [D, 1024], F32)
    wo_l = inp("wo_l", [8, 64, D], F32)
    q_tok = inp("q_tok", [1, D], F32)
    kv_tok = inp("kv_tok", [1, D], F32)
    wq_h = inp("wq_h", [D, 512], F32)
    wkv_hT = inp("wkv_hT", [D, 1024], F32)
    wo_h = inp("wo_h", [8, 64, D], F32)

    xnk = nc.dram_tensor("xnk", [T, KROW], BF16, kind="Internal").ap()
    q_dram = nc.dram_tensor("q_dram", [T, 32], BF16, kind="Internal").ap()
    s_dram = nc.dram_tensor("s_dram", [2, T], F32, kind="Internal").ap()
    idx_dram = [nc.dram_tensor("idxq_dram", [1024 + PADQ, 1], U32, kind="Internal").ap(),
                nc.dram_tensor("idxkv_dram", [2048 + PADKV, 1], U32, kind="Internal").ap()]
    heavy_dram = nc.dram_tensor("heavy_dram", [1024, D], F32, kind="Internal").ap()
    vbounce = nc.dram_tensor("vbounce", [64, 2048], BF16, kind="Internal").ap()
    msg_dram = nc.dram_tensor("msg_dram", [NS * 128, D], F32, kind="Internal").ap()

    out_packed = nc.dram_tensor("out_packed", [NS * 128, D], F32, kind="ExternalOutput").ap()
    dbg = {}
    if debug >= 1:
        dbg["xn"] = nc.dram_tensor("dbg_xn", [T, D], F32, kind="ExternalOutput").ap()
        dbg["s"] = nc.dram_tensor("dbg_s", [2, T], F32, kind="ExternalOutput").ap()
        dbg["kq"] = nc.dram_tensor("dbg_kq", [T, 64], F32, kind="ExternalOutput").ap()
        dbg["mean"] = nc.dram_tensor("dbg_mean", [128, D], F32, kind="ExternalOutput").ap()
        dbg["idxq"] = nc.dram_tensor("dbg_idxq", [1024 + PADQ, 1], U32, kind="ExternalOutput").ap()
        dbg["idxkv"] = nc.dram_tensor("dbg_idxkv", [2048 + PADKV, 1], U32, kind="ExternalOutput").ap()
        dbg["heavy"] = nc.dram_tensor("dbg_heavy", [1024, D], F32, kind="ExternalOutput").ap()
        dbg["msg"] = nc.dram_tensor("dbg_msg", [NS * 128, D], F32, kind="ExternalOutput").ap()
        dbg["smr"] = nc.dram_tensor("dbg_smr", [2 * 128, 24], F32, kind="ExternalOutput").ap()
        dbg["kap"] = nc.dram_tensor("dbg_kap", [2 * 128, 64], F32, kind="ExternalOutput").ap()
        dbg["pay"] = nc.dram_tensor("dbg_pay", [2 * 128, PADKV], U32, kind="ExternalOutput").ap()
        dbg["cross"] = nc.dram_tensor("dbg_cross", [2 * 128, 1], U32, kind="ExternalOutput").ap()
        dbg["atr"] = nc.dram_tensor("dbg_atr", [2 * 128, 64], F32, kind="ExternalOutput").ap()
        dbg["btr"] = nc.dram_tensor("dbg_btr", [2 * 128, 86], F32, kind="ExternalOutput").ap()
        dbg["dstu"] = nc.dram_tensor("dbg_dstu", [2 * 128, PADKV], U32, kind="ExternalOutput").ap()
        dbg["h_nbuf"] = nc.dram_tensor("dbg_h_nbuf", [128, 1024], F32, kind="ExternalOutput").ap()
        dbg["h_qT"] = nc.dram_tensor("dbg_h_qT", [128, 1024], F32, kind="ExternalOutput").ap()
        dbg["h_kvT"] = nc.dram_tensor("dbg_h_kvT", [128, 2048], F32, kind="ExternalOutput").ap()
        dbg["h_negb"] = nc.dram_tensor("dbg_h_negb", [128, 8], F32, kind="ExternalOutput").ap()
        dbg["h_avf"] = nc.dram_tensor("dbg_h_avf", [65, 1024], F32, kind="ExternalOutput").ap()
        dbg["h_vext"] = nc.dram_tensor("dbg_h_vext", [128, 130], F32, kind="ExternalOutput").ap()
        dbg["scm"] = nc.dram_tensor("dbg_scm", [1, 8], F32, kind="ExternalOutput").ap()

    ctx = ExitStack()
    sems, cnt = {}, {}
    for name in ["dsy", "dsc", "dpl", "pe", "act", "dve", "pool", "out"]:
        sems[name] = ctx.enter_context(nc.semaphore(name))
        cnt[name] = 0
    P = Prog(nc, sems, cnt)

    SB = {}
    def sb(name, shape, dt):
        SB[name] = ctx.enter_context(nc.sbuf_tensor(name, shape, dt))
        return SB[name]

    mean128 = sb("mean128", [128, D], F32)
    mixsc = sb("mixsc", [128, 4], F32)
    ident_bf = sb("ident_bf", [128, 128], BF16)
    ones_row_f = sb("ones_row_f", [1, 128], F32)
    ones_col_bf = sb("ones_col_bf", [128, 1], BF16)
    ones_col_f = sb("ones_col_f", [128, 1], F32)
    iota512 = sb("iota512", [128, 512], F32)
    neg9_8 = sb("neg9_8", [128, 8], F32)
    padd64f = sb("padd64f", [128, 1], F32)
    rowid_sb = sb("rowid_sb", [128, NS], F32)
    bound_sb = sb("bound_sb", [128, NS], F32)
    rowidu_sb = sb("rowidu_sb", [128, NS], U32)
    wqkT_bf = sb("wqkT_bf", [128, 8 * 64], BF16)
    sc_misc = sb("sc_misc", [1, 8], F32)
    idxq_sb = sb("idxq_sb", [128, 8], U32)
    idxkv_sb = sb("idxkv_sb", [128, 16], U32)
    ownrow_sb = sb("ownrow_sb", [128, 8], U32)
    bctx = ExitStack()
    ones_bf128 = bctx.enter_context(nc.sbuf_tensor("ones_bf128", [128, 128], BF16))
    ones128_f = bctx.enter_context(nc.sbuf_tensor("ones128_f", [128, 128], F32))
    tri_strict = bctx.enter_context(nc.sbuf_tensor("tri_strict", [128, 128], F32))
    tokf64 = bctx.enter_context(nc.sbuf_tensor("tokf64", [128, 64], F32))
    iota40 = bctx.enter_context(nc.sbuf_tensor("iota40", [128, 40], F32))
    big40 = bctx.enter_context(nc.sbuf_tensor("big40", [128, 40], F32))
    actx = ExitStack()
    ident_f = actx.enter_context(nc.sbuf_tensor("ident_f", [128, 128], F32))
    qtok128 = actx.enter_context(nc.sbuf_tensor("qtok128", [128, D], F32))
    kvtok128 = actx.enter_context(nc.sbuf_tensor("kvtok128", [128, D], F32))
    s_sb = actx.enter_context(nc.sbuf_tensor("s_sb", [128, 2 * NT], F32))
    rowtmp = [actx.enter_context(nc.sbuf_tensor(f"rowtmp{i}", [1, D], F32)) for i in range(7)]

    def p0():
        def d(sync):
            sync.dma_start(out=rowid_sb[:], in_=rowid_f).then_inc(sems["dsy"], 16)
            sync.dma_start(out=bound_sb[:], in_=bound_f).then_inc(sems["dsy"], 16)
            sync.dma_start(out=rowidu_sb[:], in_=rowid_u).then_inc(sems["dsy"], 16)
            for i, (srcap, dstrow) in enumerate([(gain_in, 0), (bias_in, 1), (gam_l, 2), (gam_h, 3),
                                               (q_tok, 4), (kv_tok, 5)]):
                sync.dma_start(out=rowtmp[i][:], in_=srcap).then_inc(sems["dsy"], 16)
            sync.dma_start(out=sc_misc[:, 0:1], in_=log_mix).then_inc(sems["dsy"], 16)
            return sync.dma_start(out=sc_misc[:, 1:2], in_=log_scale)
        P.op("sync", d, incs=[("dsy", 16)])
        cnt["dsy"] += 16 * 10

        def casts(gp):
            gp.dma_start(out=wqkT_bf[:].rearrange("p (c n) -> p c n", c=8)[:, :, 0:32], in_=wkT.rearrange("(c p) n -> p c n", p=128)).then_inc(sems["dpl"], 16)
            return gp.dma_start(out=wqkT_bf[:].rearrange("p (c n) -> p c n", c=8)[:, :, 32:64], in_=wqT.rearrange("(c p) n -> p c n", p=128))
        P.op("gpsimd", casts, incs=[("dpl", 16)])
        cnt["dpl"] += 16

        def consts_pool(gp):
            gp.iota(iota512[:], pattern=[[1, 512]], base=0, channel_multiplier=0, allow_small_or_imprecise_dtypes=True)
            gp.iota(tokf64[:], pattern=[[1, 64]], base=0, channel_multiplier=64, allow_small_or_imprecise_dtypes=True)
            gp.iota(padd64f[:], pattern=[[1, 1]], base=0, channel_multiplier=64, allow_small_or_imprecise_dtypes=True)
            gp.iota(iota40[:], pattern=[[1, 40]], base=0, channel_multiplier=0, allow_small_or_imprecise_dtypes=True)
            gp.memset(big40[:], 1.0e9)
            gp.memset(neg9_8[:], NEG9)
            gp.memset(ones128_f[:], 1.0)
            return gp.memset(ones_bf128[:], 1.0)
        P.op("gpsimd", consts_pool, incs=[("pool", 1)])

        def consts_dve(v):
            v.memset(ones_row_f[:], 1.0)
            v.memset(ones_col_bf[:], 1.0)
            return v.memset(ones_col_f[:], 1.0)
        P.op("vector", consts_dve, incs=[("dve", 1)])

        def aff(gp):
            gp.affine_select(out=ident_bf[:], in_=ones_bf128[:], pattern=[[-1, 128]],
                             compare_op=ALU.is_equal, fill=0.0, base=0, channel_multiplier=1)
            gp.affine_select(out=ident_f[:], in_=ones128_f[:], pattern=[[-1, 128]],
                             compare_op=ALU.is_equal, fill=0.0, base=0, channel_multiplier=1)
            return gp.affine_select(out=tri_strict[:], in_=ones128_f[:], pattern=[[1, 128]],
                                    compare_op=ALU.is_gt, fill=0.0, base=0, channel_multiplier=-1)
        P.op("gpsimd", aff, waits=[("pool", None)], incs=[("pool", 1)])

        def scalars(sc):
            sc.activation(out=sc_misc[:, 2:3], in_=sc_misc[:, 0:1], func=AF.Sigmoid)
            sc.activation(out=sc_misc[:, 3:4], in_=sc_misc[:, 1:2], func=AF.Abs)
            sc.activation(out=sc_misc[:, 4:5], in_=sc_misc[:, 3:4], func=AF.Exp, scale=-1.0)
            return sc.activation(out=sc_misc[:, 5:6], in_=sc_misc[:, 4:5], func=AF.Ln, bias=1.0)
        P.op("scalar", scalars, waits=[("dsy", None)], incs=[("act", 1)])

        P.op("scalar", lambda sc: sc.activation(out=sc_misc[:, 7:8], in_=sc_misc[:, 1:2], func=AF.Relu),
             waits=[("act", None)], incs=[("act", 1)])
        P.op("scalar", lambda sc: sc.activation(out=sc_misc[:, 6:7], in_=sc_misc[:, 5:6], func=AF.Identity, bias=sc_misc[:, 7:8]),
             waits=[("act", None)], incs=[("act", 1)])

        if debug >= 2:
            P.op("gpsimd", lambda gp: gp.dma_start(out=dbg["scm"], in_=sc_misc[:]), waits=[("dve", None), ("act", None)], incs=[("dpl", 16)])
        P.flush("p0")

    def p0b():
        pctx = ExitStack()
        ps_bc = pctx.enter_context(nc.psum_tensor("ps_bc", [128, 512], F32))
        ps_bc2 = pctx.enter_context(nc.psum_tensor("ps_bc2", [128, 512], F32))
        ps_sc = pctx.enter_context(nc.psum_tensor("ps_sc", [128, 4], F32))
        targets = [(rowtmp[4], qtok128), (rowtmp[5], kvtok128)]
        for i, (src, dst) in enumerate(targets):
            def mm(te, src=src):
                te.matmul(ps_bc[:], lhsT=ones_row_f[:], rhs=src[0:1, 0:512], start=True, stop=True)
                return te.matmul(ps_bc2[:], lhsT=ones_row_f[:], rhs=src[0:1, 512:1024], start=True, stop=True)
            P.op("tensor", mm, waits=[("act", None), ("dsy", None)], incs=[("pe", 1)])
            def ev(sc, dst=dst):
                sc.copy(out=dst[:, 0:512], in_=ps_bc[:])
                return sc.copy(out=dst[:, 512:1024], in_=ps_bc2[:])
            P.op("scalar", ev, waits=[("pe", None)], incs=[("act", 1)])
            if i < len(targets) - 1:
                P.op("tensor", lambda te: te.nop(), waits=[("act", None)], incs=[])
        def mmsc(te):
            return te.matmul(ps_sc[:, 0:4], lhsT=ones_row_f[:], rhs=sc_misc[0:1, 2:6], start=True, stop=True)
        P.op("tensor", mmsc, waits=[("dve", None), ("act", None)], incs=[("pe", 1)])
        def evsc(v):
            v.tensor_copy(out=mixsc[:, 0:1], in_=ps_sc[:, 0:1])
            v.tensor_copy(out=mixsc[:, 2:3], in_=ps_sc[:, 0:1])
            v.tensor_scalar(out=mixsc[:, 1:2], in0=ps_sc[:, 0:1], scalar1=-1.0, scalar2=1.0, op0=ALU.mult, op1=ALU.add)
            return v.tensor_copy(out=mixsc[:, 3:4], in_=ps_sc[:, 0:1])
        P.op("vector", evsc, waits=[("pe", None)], incs=[("dve", 1)])
        P.flush("p0b")
        pctx.close()

    def p0c():
        pctx = ExitStack()
        ps_sc2 = pctx.enter_context(nc.psum_tensor("ps_sc2", [128, 1], F32))
        def mmsc2(te):
            return te.matmul(ps_sc2[:], lhsT=ones_row_f[:], rhs=sc_misc[0:1, 6:7], start=True, stop=True)
        P.op("tensor", mmsc2, waits=[("dve", None), ("act", None)], incs=[("pe", 1)])
        def evsc2(v):
            return v.tensor_scalar(out=mixsc[:, 2:3], in0=ps_sc2[:], scalar1=0.01, scalar2=None, op0=ALU.add)
        P.op("vector", evsc2, waits=[("pe", None)], incs=[("dve", 1)])
        P.flush("p0c")
        pctx.close()

    def p1():
        pctx = ExitStack()
        xb = [pctx.enter_context(nc.sbuf_tensor(f"xb{i}", [128, D], F32)) for i in range(2)]
        xc = [pctx.enter_context(nc.sbuf_tensor(f"xc{i}", [128, D], F32)) for i in range(2)]
        xnb = [pctx.enter_context(nc.sbuf_tensor(f"xnb{i}", [128, D], BF16)) for i in range(2)]
        sqs = [pctx.enter_context(nc.sbuf_tensor(f"sqs{i}", [128, D], F32)) for i in range(2)]
        st_small = pctx.enter_context(nc.sbuf_tensor("st_small", [128, 16], F32))
        ps_mean = [pctx.enter_context(nc.psum_tensor(f"ps_mean{i}", [1, 512], F32)) for i in range(2)]

        act_done_tile = {}
        dve_done_tile = {}
        for i in range(NT):
            j = i % 2
            wl = [("act", act_done_tile.get(i - 2, 0))]
            P.op("sync", lambda sy, i=i, j=j: sy.dma_start(out=xb[j][:], in_=x_all[i * 128:(i + 1) * 128, :]),
                 waits=wl, incs=[("dsy", 16)])
            P.op("vector", lambda v, j=j: v.reduce_sum(out=st_small[:, 0 + j:1 + j], in_=xb[j][:], axis=AX.X, negate=True),
                 waits=[("dsy", None), ("dve", dve_done_tile.get(i - 2, 0))], incs=[("dve", 1)])
            P.op("vector", lambda v, j=j: v.tensor_scalar(out=st_small[:, 0 + j:1 + j], in0=st_small[:, 0 + j:1 + j],
                                                          scalar1=1.0 / 1024.0, scalar2=None, op0=ALU.mult),
                 waits=[("dve", None)], incs=[("dve", 1)])
            P.op("scalar", lambda sc, j=j: sc.activation(out=xc[j][:], in_=xb[j][:], func=AF.Identity, bias=st_small[:, 0 + j:1 + j]),
                 waits=[("dve", None)], incs=[("act", 1)])
            act_done_tile[i] = cnt["act"]
            P.op("scalar", lambda sc, j=j: sc.activation(out=sqs[j][:], in_=xc[j][:], func=AF.Square, accum_out=st_small[:, 2 + j:3 + j]),
                 waits=[("act", None)], incs=[("act", 1)])
            P.op("vector", lambda v, j=j: v.tensor_scalar(out=st_small[:, 6 + j:7 + j], in0=st_small[:, 2 + j:3 + j],
                                                          scalar1=1.0 / 1024.0, scalar2=1e-5, op0=ALU.mult, op1=ALU.add),
                 waits=[("act", None)], incs=[("dve", 1)])
            P.op("scalar", lambda sc, j=j: sc.sqrt(out=st_small[:, 6 + j:7 + j], in_=st_small[:, 6 + j:7 + j]),
                 waits=[("dve", None)], incs=[("act", 1)])
            P.op("vector", lambda v, j=j: v.reciprocal(out=st_small[:, 4 + j:5 + j], in_=st_small[:, 6 + j:7 + j]),
                 waits=[("act", None)], incs=[("dve", 1)])
            P.op("scalar", lambda sc, j=j: sc.activation(out=xnb[j][:], in_=xc[j][:], func=AF.Identity, scale=st_small[:, 4 + j:5 + j]),
                 waits=[("dve", None)], incs=[("act", 1)])
            P.op("scalar", lambda sc, i=i, j=j: sc.dma_start(out=xnk[i * 128:(i + 1) * 128, 0:1024], in_=xnb[j][:]),
                 waits=[("act", None)], incs=[("dsc", 16)])
            if debug >= 1:
                P.op("gpsimd", lambda gp, i=i, j=j: gp.dma_start(out=dbg["xn"][i * 128:(i + 1) * 128, :], in_=xnb[j][:]),
                     waits=[("act", None)], incs=[("dpl", 16)])
            P.op("vector", lambda v, j=j: v.tensor_mul(out=sqs[j][:], in0=xc[j][:], in1=qtok128[:]),
                 waits=[("act", None)], incs=[("dve", 1)])
            P.op("vector", lambda v, j=j: v.reduce_sum(out=st_small[:, 8 + j:9 + j], in_=sqs[j][:], axis=AX.X),
                 waits=[("dve", None)], incs=[("dve", 1)])
            P.op("vector", lambda v, i=i, j=j: v.tensor_mul(out=s_sb[:, i:i + 1], in0=st_small[:, 8 + j:9 + j], in1=st_small[:, 4 + j:5 + j]),
                 waits=[("dve", None)], incs=[("dve", 1)])
            dve_q_done = cnt["dve"]
            P.op("gpsimd", lambda gp, j=j: gp.tensor_mul(out=xc[j][:], in0=xc[j][:], in1=kvtok128[:]),
                 waits=[("dve", dve_q_done)], incs=[("pool", 1)])
            P.op("vector", lambda v, j=j: v.tensor_reduce(out=st_small[:, 10 + j:11 + j], in_=xc[j][:], axis=AX.X, op=ALU.add),
                 waits=[("pool", None)], incs=[("dve", 1)])
            P.op("gpsimd", lambda gp, i=i, j=j: gp.tensor_mul(out=s_sb[:, NT + i:NT + i + 1], in0=st_small[:, 10 + j:11 + j], in1=st_small[:, 4 + j:5 + j]),
                 waits=[("dve", None)], incs=[("pool", 1)])
            dve_done_tile[i] = cnt["pool"]
            P.op("tensor", lambda te, i=i, j=j: te.matmul(ps_mean[0][:], lhsT=ones_col_bf[:], rhs=xnb[j][:, 0:512],
                                                          start=(i == 0), stop=(i == NT - 1)),
                 waits=[("act", None)], incs=[("pe", 1)])
            P.op("tensor", lambda te, i=i, j=j: te.matmul(ps_mean[1][:], lhsT=ones_col_bf[:], rhs=xnb[j][:, 512:1024],
                                                          start=(i == 0), stop=(i == NT - 1)),
                 incs=[("pe", 1)])
            act_done_tile[i] = cnt["act"]
            dve_done_tile[i] = cnt["dve"]
        P.op("scalar", lambda sc: sc.activation(out=rowtmp[6][0:1, 0:512], in_=ps_mean[0][:], func=AF.Identity, scale=1.0 / 8192.0),
             waits=[("pe", None)], incs=[("act", 1)])
        P.op("scalar", lambda sc: sc.activation(out=rowtmp[6][0:1, 512:1024], in_=ps_mean[1][:], func=AF.Identity, scale=1.0 / 8192.0),
             incs=[("act", 1)])
        P.flush("p1")
        pctx.close()

        pctx = ExitStack()
        ps_b1 = pctx.enter_context(nc.psum_tensor("ps_b1", [128, 512], F32))
        ps_b2 = pctx.enter_context(nc.psum_tensor("ps_b2", [128, 512], F32))
        ps_str = pctx.enter_context(nc.psum_tensor("ps_str", [64, 128], F32))
        sT = pctx.enter_context(nc.sbuf_tensor("sT", [64, 128], F32))
        def mmm(te):
            te.matmul(ps_b1[:], lhsT=ones_row_f[:], rhs=rowtmp[6][0:1, 0:512], start=True, stop=True)
            return te.matmul(ps_b2[:], lhsT=ones_row_f[:], rhs=rowtmp[6][0:1, 512:1024], start=True, stop=True)
        P.op("tensor", mmm, incs=[("pe", 1)])
        def evm(sc):
            sc.copy(out=mean128[:, 0:512], in_=ps_b1[:])
            return sc.copy(out=mean128[:, 512:1024], in_=ps_b2[:])
        P.op("scalar", evm, waits=[("pe", None)], incs=[("act", 1)])
        if debug >= 1:
            P.op("scalar", lambda sc: sc.dma_start(out=dbg["mean"], in_=mean128[:]),
                 waits=[("act", None)], incs=[("dsc", 16)])
        for r in range(2):
            P.op("tensor", lambda te, r=r: te.transpose(ps_str[:], in_=s_sb[:, r * NT:(r + 1) * NT], identity=ident_f[:]),
                 incs=[("pe", 1)])
            P.op("vector", lambda v: v.tensor_copy(out=sT[:], in_=ps_str[:]),
                 waits=[("pe", None), ("dsy", None)], incs=[("dve", 1)])
            P.op("sync", lambda sy, r=r: sy.dma_start(out=s_dram[r:r + 1, :].rearrange("one (i p) -> (one i) p", p=128), in_=sT[:]),
                 waits=[("dve", None)], incs=[("dsy", 16)])
            P.op("tensor", lambda te: te.nop(), waits=[("dve", None)], incs=[])
        if debug >= 1:
            P.op("sync", lambda sy: sy.dma_start(out=dbg["s"], in_=s_dram), waits=[("dsy", None)], incs=[("dsy", 16)])
        P.flush("p1T")
        pctx.close()

    def p2():
        pctx = ExitStack()
        srt = [pctx.enter_context(nc.sbuf_tensor(f"srt{r}", [128, 64], F32)) for r in range(2)]
        bneg = [pctx.enter_context(nc.sbuf_tensor(f"bneg{r}", [128, 64], F32)) for r in range(2)]
        ubuf = [pctx.enter_context(nc.sbuf_tensor(f"ubuf{r}", [128, 64], F32)) for r in range(2)]
        ebuf = [pctx.enter_context(nc.sbuf_tensor(f"ebuf{r}", [128, 64], F32)) for r in range(2)]
        kap = [pctx.enter_context(nc.sbuf_tensor(f"kap{r}", [128, 64], F32)) for r in range(2)]
        kapx = [pctx.enter_context(nc.sbuf_tensor(f"kapx{r}", [128, 64], F32)) for r in range(2)]
        smr = [pctx.enter_context(nc.sbuf_tensor(f"smr{r}", [128, 24], F32)) for r in range(2)]
        vals8b = [pctx.enter_context(nc.sbuf_tensor(f"vals8b{r}", [128, 8], F32)) for r in range(2)]
        pos8b = [pctx.enter_context(nc.sbuf_tensor(f"pos8b{r}", [128, 8], U32)) for r in range(2)]
        posf = [pctx.enter_context(nc.sbuf_tensor(f"posf{r}", [128, 8], F32)) for r in range(2)]
        ltb = [pctx.enter_context(nc.sbuf_tensor(f"ltb{r}", [128, 1], U8)) for r in range(2)]
        geb = [pctx.enter_context(nc.sbuf_tensor(f"geb{r}", [128, 1], U8)) for r in range(2)]
        vmask = [pctx.enter_context(nc.sbuf_tensor(f"vmask{r}", [128, 8], U8)) for r in range(2)]
        vmask40 = [pctx.enter_context(nc.sbuf_tensor(f"vmask40{r}", [128, PADKV], U8)) for r in range(2)]
        destu = [pctx.enter_context(nc.sbuf_tensor(f"destu{r}", [128, PADKV], U32)) for r in range(2)]
        paycol = [pctx.enter_context(nc.sbuf_tensor(f"paycol{r}", [128, 4], U32)) for r in range(2)]
        destf2 = [pctx.enter_context(nc.sbuf_tensor(f"destf2{r}", [128, PADKV], F32)) for r in range(2)]
        payf = [pctx.enter_context(nc.sbuf_tensor(f"payf{r}", [128, PADKV], F32)) for r in range(2)]
        payu = [pctx.enter_context(nc.sbuf_tensor(f"payu{r}", [128, PADKV], U32)) for r in range(2)]
        crossu = [pctx.enter_context(nc.sbuf_tensor(f"crossu{r}", [128, 1], U32)) for r in range(2)]
        cmpb = [pctx.enter_context(nc.sbuf_tensor(f"cmpb{r}", [128, 64], F32)) for r in range(2)]
        zrow = pctx.enter_context(nc.sbuf_tensor("zrow", [1, 2112], U32))
        ps_tot = [pctx.enter_context(nc.psum_tensor(f"ps_tot{r}", [128, 1], F32)) for r in range(2)]

        KK = [1024.0, 2048.0]
        LOGK = [LOGK_Q, LOGK_KV]
        PAD = [PADQ, PADKV]
        NRND = [3, 5]
        ENG = ["vector", "vector"]
        SEMN = ["dve", "dve"]

        atr = [pctx.enter_context(nc.sbuf_tensor(f"atr{r}", [128, 64], F32)) for r in range(2)]
        btr = [pctx.enter_context(nc.sbuf_tensor(f"btr{r}", [128, 86], F32)) for r in range(2)]
        for r in range(2):
            P.op("sync", lambda sy, r=r: sy.dma_start(out=srt[r][:], in_=s_dram[r:r + 1, :].rearrange("one (p i) -> (one p) i", p=128)),
                 incs=[("dsy", 16)])
            P.op(ENG[r], lambda h, r=r: h.tensor_copy(out=bneg[r][:], in_=srt[r][:]),
                 waits=[("dsy", None)], incs=[(SEMN[r], 1)])
        P.op("vector", lambda v: v.memset(zrow[:], 0), incs=[("dve", 1)])
        for r in range(2):
            P.op("sync", lambda sy, r=r: sy.dma_start(out=idx_dram[r].rearrange("n one -> (one) (n)"), in_=zrow[0:1, 0:(1024, 2048)[r] + PAD[r]]),
                 waits=[("dve", None)], incs=[("dsy", 16)])

        for it in range(50):
            for r in range(2):
                e, sn = ENG[r], SEMN[r]
                P.op(e, lambda h, r=r: h.tensor_sub(out=ubuf[r][:], in0=srt[r][:], in1=bneg[r][:]),
                     waits=[(sn, None)], incs=[(sn, 1)])
                P.op("scalar", lambda sc, r=r: sc.activation(out=ebuf[r][:], in_=ubuf[r][:], func=AF.Exp),
                     waits=[(sn, None)], incs=[("act", 1)])
                P.op(e, lambda h, r=r: h.tensor_reduce(out=smr[r][:, 0:1], in_=ebuf[r][:], axis=AX.X, op=ALU.add),
                     waits=[(sn, None), ("act", None)], incs=[(sn, 1)])
                P.op(e, lambda h, r=r: h.tensor_copy(out=smr[r][:, 12:13], in_=smr[r][:, 0:1]),
                     waits=[(sn, None)], incs=[(sn, 1)])
                P.op("tensor", lambda te, r=r: te.matmul(ps_tot[r][:], lhsT=ones128_f[:], rhs=smr[r][:, 0:1], start=True, stop=True),
                     waits=[(sn, None), ("act", None)], incs=[("pe", 1)])
                P.op("scalar", lambda sc, r=r: sc.activation(out=smr[r][:, 2:3], in_=ps_tot[r][:], func=AF.Ln),
                     waits=[("pe", None)], incs=[("act", 1)])
                P.op("scalar", lambda sc, r=r: sc.copy(out=smr[r][:, 13:14], in_=smr[r][:, 2:3]),
                     incs=[("act", 1)])
                P.op(e, lambda h, r=r: h.tensor_scalar(out=smr[r][:, 3:4], in0=smr[r][:, 2:3], scalar1=-1.0, scalar2=LOGK[r], op0=ALU.mult, op1=ALU.add),
                     waits=[(sn, None), ("act", None)], incs=[(sn, 1)])
                P.op(e, lambda h, r=r: h.tensor_scalar(out=bneg[r][:], in0=srt[r][:], scalar1=smr[r][:, 3:4], scalar2=0.0, op0=ALU.add, op1=ALU.max),
                     waits=[(sn, None)], incs=[(sn, 1)])
                if debug >= 2 and it < 64:
                    P.op(e, lambda h, r=r, it=it: h.tensor_copy(out=atr[r][:, it:it + 1], in_=smr[r][:, 3:4]), waits=[(sn, None)], incs=[(sn, 1)])
        for r in range(2):
            e, sn = ENG[r], SEMN[r]
            P.op(e, lambda h, r=r: h.tensor_scalar(out=ubuf[r][:], in0=srt[r][:], scalar1=smr[r][:, 3:4], scalar2=None, op0=ALU.add),
                 waits=[(sn, None)], incs=[(sn, 1)])
            P.op(e, lambda h, r=r: h.tensor_add(out=ebuf[r][:], in0=tokf64[:], in1=ubuf[r][:]),
                 waits=[(sn, None)], incs=[(sn, 1)])
            P.op(e, lambda h, r=r: h.tensor_scalar(out=ebuf[r][:], in0=ebuf[r][:], scalar1=-1.0, scalar2=16384.0, op0=ALU.mult, op1=ALU.add),
                 waits=[(sn, None)], incs=[(sn, 1)])
            P.op(e, lambda h, r=r: h.tensor_scalar(out=cmpb[r][:], in0=ubuf[r][:], scalar1=0.0, scalar2=None, op0=ALU.is_ge),
                 waits=[(sn, None)], incs=[(sn, 1)])
            P.op(e, lambda h, r=r: h.tensor_mul(out=kap[r][:], in0=cmpb[r][:], in1=ebuf[r][:]),
                 waits=[(sn, None)], incs=[(sn, 1)])
            P.op(e, lambda h, r=r: h.tensor_scalar(out=ebuf[r][:], in0=cmpb[r][:], scalar1=-1.0, scalar2=1.0, op0=ALU.mult, op1=ALU.add),
                 waits=[(sn, None)], incs=[(sn, 1)])
            P.op(e, lambda h, r=r: h.tensor_mul(out=ebuf[r][:], in0=ebuf[r][:], in1=ubuf[r][:]),
                 waits=[(sn, None)], incs=[(sn, 1)])
            P.op(e, lambda h, r=r: h.tensor_add(out=kap[r][:], in0=kap[r][:], in1=ebuf[r][:]),
                 waits=[(sn, None)], incs=[(sn, 1)])
            P.op(e, lambda h, r=r: h.tensor_copy(out=kapx[r][:], in_=kap[r][:]),
                 waits=[(sn, None)], incs=[(sn, 1)])
            P.op(e, lambda h, r=r: h.tensor_scalar(out=smr[r][:, 4:5], in0=smr[r][:, 3:4], scalar1=0.0, scalar2=-2048.0, op0=ALU.mult, op1=ALU.add), waits=[(sn, None)], incs=[(sn, 1)])
            P.op(e, lambda h, r=r: h.tensor_scalar(out=smr[r][:, 5:6], in0=smr[r][:, 3:4], scalar1=0.0, scalar2=16384.0, op0=ALU.mult, op1=ALU.add), waits=[(sn, None)], incs=[(sn, 1)])
            P.op(e, lambda h, r=r: h.tensor_copy(out=btr[r][:, 84:85], in_=smr[r][:, 4:5]), waits=[(sn, None), (sn, None)], incs=[(sn, 1)])
            P.op(e, lambda h, r=r: h.tensor_copy(out=btr[r][:, 85:86], in_=smr[r][:, 5:6]), waits=[(sn, None)], incs=[(sn, 1)])
        for rnd in range(NBIS):
            for r in range(2):
                e, sn = ENG[r], SEMN[r]
                P.op(e, lambda h, r=r: h.tensor_add(out=smr[r][:, 6:7], in0=smr[r][:, 4:5], in1=smr[r][:, 5:6]),
                     waits=[(sn, None)], incs=[(sn, 1)])
                P.op(e, lambda h, r=r: h.tensor_scalar(out=smr[r][:, 6:7], in0=smr[r][:, 6:7], scalar1=0.5, scalar2=None, op0=ALU.mult),
                     waits=[(sn, None)], incs=[(sn, 1)])
                P.op(e, lambda h, r=r: h.tensor_scalar(out=cmpb[r][:], in0=kap[r][:], scalar1=smr[r][:, 6:7], scalar2=None, op0=ALU.is_gt),
                     waits=[(sn, None)], incs=[(sn, 1)])
                P.op(e, lambda h, r=r: h.tensor_reduce(out=smr[r][:, 7:8], in_=cmpb[r][:], axis=AX.X, op=ALU.add),
                     waits=[(sn, None)], incs=[(sn, 1)])
                P.op("tensor", lambda te, r=r: te.matmul(ps_tot[r][:], lhsT=ones128_f[:], rhs=smr[r][:, 7:8], start=True, stop=True),
                     waits=[(sn, None)], incs=[("pe", 1)])
                P.op("scalar", lambda sc, r=r: sc.copy(out=smr[r][:, 7:8], in_=ps_tot[r][:]),
                     waits=[("pe", None)], incs=[("act", 1)])
                P.op(e, lambda h, r=r: h.tensor_scalar(out=geb[r][:], in0=smr[r][:, 7:8], scalar1=KK[r], scalar2=None, op0=ALU.is_ge),
                     waits=[(sn, None), ("act", None)], incs=[(sn, 1)])
                P.op(e, lambda h, r=r: h.tensor_scalar(out=ltb[r][:], in0=smr[r][:, 7:8], scalar1=KK[r], scalar2=None, op0=ALU.is_lt),
                     waits=[(sn, None)], incs=[(sn, 1)])
                if debug >= 2:
                    P.op(e, lambda h, r=r, rnd=rnd: h.tensor_copy(out=btr[r][:, 2 * rnd:2 * rnd + 1], in_=smr[r][:, 6:7]), waits=[(sn, None)], incs=[(sn, 1)])
                    P.op(e, lambda h, r=r, rnd=rnd: h.tensor_copy(out=btr[r][:, 2 * rnd + 1:2 * rnd + 2], in_=smr[r][:, 7:8]), waits=[(sn, None)], incs=[(sn, 1)])
                P.op(e, lambda h, r=r: h.select(out=smr[r][:, 4:5], mask=geb[r][:], on_true=smr[r][:, 6:7], on_false=smr[r][:, 4:5]),
                     waits=[(sn, None)], incs=[(sn, 1)])
                P.op(e, lambda h, r=r: h.select(out=smr[r][:, 5:6], mask=ltb[r][:], on_true=smr[r][:, 6:7], on_false=smr[r][:, 5:6]),
                     waits=[(sn, None)], incs=[(sn, 1)])
        for r in range(2):
            e, sn = ENG[r], SEMN[r]
            P.op(e, lambda h, r=r: h.tensor_scalar(out=cmpb[r][:], in0=kap[r][:], scalar1=smr[r][:, 4:5], scalar2=None, op0=ALU.is_gt),
                 waits=[(sn, None)], incs=[(sn, 1)])
            P.op(e, lambda h, r=r: h.tensor_reduce(out=smr[r][:, 9:10], in_=cmpb[r][:], axis=AX.X, op=ALU.add),
                 waits=[(sn, None)], incs=[(sn, 1)])
            P.op("tensor", lambda te, r=r: te.matmul(ps_tot[r][:], lhsT=tri_strict[:], rhs=smr[r][:, 9:10], start=True, stop=True),
                 waits=[(sn, None)], incs=[("pe", 1)])
            P.op("scalar", lambda sc, r=r: sc.copy(out=smr[r][:, 10:11], in_=ps_tot[r][:]),
                 waits=[("pe", None)], incs=[("act", 1)])
            P.op(e, lambda h, r=r: h.tensor_copy(out=crossu[r][:], in_=smr[r][:, 10:11]),
                 waits=[(sn, None), ("act", None)], incs=[(sn, 1)])
            for rnd in range(NRND[r]):
                P.op(e, lambda h, r=r: h.max(out=vals8b[r][:], in_=kapx[r][:]), waits=[(sn, None)], incs=[(sn, 1)])
                P.op(e, lambda h, r=r: h.max_index(out=pos8b[r][:], in_max=vals8b[r][:], in_values=kapx[r][:]),
                     waits=[(sn, None)], incs=[(sn, 1)])
                P.op(e, lambda h, r=r: h.match_replace(out=kapx[r][:], in_to_replace=vals8b[r][:], in_values=kapx[r][:], imm_value=NEG30),
                     waits=[(sn, None)], incs=[(sn, 1)])
                P.op(e, lambda h, r=r: h.tensor_copy(out=posf[r][:], in_=pos8b[r][:]),
                     waits=[(sn, None)], incs=[(sn, 1)])
                P.op(e, lambda h, r=r: h.tensor_scalar(out=posf[r][:], in0=posf[r][:], scalar1=padd64f[:], scalar2=None, op0=ALU.add),
                     waits=[(sn, None)], incs=[(sn, 1)])
                P.op(e, lambda h, r=r: h.tensor_scalar(out=vmask[r][:], in0=vals8b[r][:], scalar1=smr[r][:, 4:5], scalar2=None, op0=ALU.is_gt),
                     waits=[(sn, None)], incs=[(sn, 1)])
                P.op(e, lambda h, r=r, rnd=rnd: h.select(out=payf[r][:, rnd * 8:(rnd + 1) * 8], mask=vmask[r][:], on_true=posf[r][:], on_false=neg9_8[:]),
                     waits=[(sn, None)], incs=[(sn, 1)])
            P.op(e, lambda h, r=r: h.tensor_scalar(out=payf[r][:, 0:PAD[r]], in0=payf[r][:, 0:PAD[r]], scalar1=0.0, scalar2=None, op0=ALU.max),
                 waits=[(sn, None)], incs=[(sn, 1)])
            P.op(e, lambda h, r=r: h.tensor_copy(out=payu[r][:, 0:PAD[r]], in_=payf[r][:, 0:PAD[r]]),
                 waits=[(sn, None)], incs=[(sn, 1)])
            P.op(e, lambda h, r=r: h.tensor_scalar(out=payf[r][:, 0:PAD[r]], in0=iota40[:, 0:PAD[r]], scalar1=smr[r][:, 10:11], scalar2=None, op0=ALU.add),
                 waits=[(sn, None)], incs=[(sn, 1)])
            P.op(e, lambda h, r=r: h.tensor_scalar(out=vmask[r][:], in0=iota40[:, 0:8], scalar1=smr[r][:, 9:10], scalar2=None, op0=ALU.is_lt),
                 waits=[(sn, None)], incs=[(sn, 1)]) if False else None
            P.op(e, lambda h, r=r: h.tensor_scalar(out=vmask40[r][:, 0:PAD[r]], in0=iota40[:, 0:PAD[r]], scalar1=smr[r][:, 9:10], scalar2=None, op0=ALU.is_lt),
                 waits=[(sn, None)], incs=[(sn, 1)])
            P.op(e, lambda h, r=r: h.select(out=destf2[r][:, 0:PAD[r]], mask=vmask40[r][:, 0:PAD[r]], on_true=payf[r][:, 0:PAD[r]], on_false=big40[:, 0:PAD[r]]),
                 waits=[(sn, None)], incs=[(sn, 1)])
            P.op(e, lambda h, r=r: h.tensor_scalar(out=destu[r][:, 0:PAD[r]], in0=destf2[r][:, 0:PAD[r]], scalar1=float((1024, 2048)[r] + PAD[r] - 1), scalar2=None, op0=ALU.min),
                 waits=[(sn, None)], incs=[(sn, 1)])
            for j in range(PAD[r]):
                P.op(e, lambda h, r=r, j=j: h.tensor_copy(out=paycol[r][:, (j % 4):(j % 4) + 1], in_=payu[r][:, j:j + 1]),
                     waits=[("dpl", cnt["dpl"] - 16 * 3 if j >= 4 else 0)], incs=[(sn, 1)])
                P.op("gpsimd", lambda gp, r=r, j=j: gp.indirect_dma_start(
                        out=idx_dram[r], out_offset=IndirectOffsetOnAxis(destu[r][:, j:j + 1], 0),
                        in_=paycol[r][:, (j % 4):(j % 4) + 1], in_offset=None),
                     waits=[(sn, None), ("dsy", None)], incs=[("dpl", 16)])
        if debug >= 1:
            P.op("sync", lambda sy: sy.dma_start(out=dbg["idxq"], in_=idx_dram[0]), waits=[("dpl", None)], incs=[("dsy", 16)])
            P.op("sync", lambda sy: sy.dma_start(out=dbg["idxkv"], in_=idx_dram[1]), waits=[("dpl", None)], incs=[("dsy", 16)])
            for r in (range(2) if debug >= 2 else []):
                P.op("sync", lambda sy, r=r: sy.dma_start(out=dbg["smr"][r * 128:(r + 1) * 128, :], in_=smr[r][:]), incs=[("dsy", 16)])
                P.op("sync", lambda sy, r=r: sy.dma_start(out=dbg["kap"][r * 128:(r + 1) * 128, :], in_=kap[r][:]), incs=[("dsy", 16)])
                P.op("sync", lambda sy, r=r: sy.dma_start(out=dbg["pay"][r * 128:(r + 1) * 128, :], in_=payu[r][:]), incs=[("dsy", 16)])
                P.op("sync", lambda sy, r=r: sy.dma_start(out=dbg["cross"][r * 128:(r + 1) * 128, :], in_=crossu[r][:]), incs=[("dsy", 16)])
                P.op("sync", lambda sy, r=r: sy.dma_start(out=dbg["atr"][r * 128:(r + 1) * 128, :], in_=atr[r][:]), incs=[("dsy", 16)])
                P.op("sync", lambda sy, r=r: sy.dma_start(out=dbg["btr"][r * 128:(r + 1) * 128, :], in_=btr[r][:]), incs=[("dsy", 16)])
                P.op("sync", lambda sy, r=r: sy.dma_start(out=dbg["dstu"][r * 128:(r + 1) * 128, :], in_=destu[r][:]), incs=[("dsy", 16)])
        P.flush("p2")
        pctx.close()


    def bcast_row(src_ap, dst, ps_a, ps_b, rowbuf):
        P.op("sync", lambda sy: sy.dma_start(out=rowbuf[:], in_=src_ap), waits=[("pe", None)], incs=[("dsy", 16)])
        def mmb(te):
            te.matmul(ps_a[:], lhsT=ones_row_f[:], rhs=rowbuf[0:1, 0:512], start=True, stop=True)
            return te.matmul(ps_b[:], lhsT=ones_row_f[:], rhs=rowbuf[0:1, 512:1024], start=True, stop=True)
        P.op("tensor", mmb, waits=[("dsy", None), ("act", None)], incs=[("pe", 1)])
        def evb(sc):
            sc.copy(out=dst[:, 0:512], in_=ps_a[:])
            return sc.copy(out=dst[:, 512:1024], in_=ps_b[:])
        P.op("scalar", evb, waits=[("pe", None)], incs=[("act", 1)])
        P.op("tensor", lambda te: te.nop(), waits=[("act", None)], incs=[])
    def p3():
        pctx = ExitStack()
        wqh_bf = pctx.enter_context(nc.sbuf_tensor("wqh_bf", [128, 8 * 512], BF16))
        wkvhT_bf = pctx.enter_context(nc.sbuf_tensor("wkvhT_bf", [128, 8 * 1024], BF16))
        woh_bf = pctx.enter_context(nc.sbuf_tensor("woh_bf", [64, 8 * 1024], BF16))
        gamH128 = pctx.enter_context(nc.sbuf_tensor("gamH128", [128, D], BF16))
        rowbuf3 = pctx.enter_context(nc.sbuf_tensor("rowbuf3", [1, D], F32))
        kvT = pctx.enter_context(nc.sbuf_tensor("kvT", [128, 8 * 2048], BF16))
        qTpad = [pctx.enter_context(nc.sbuf_tensor(f"qTpad{h}", [128, 1024], BF16)) for h in range(8)]
        hsm = pctx.enter_context(nc.sbuf_tensor("hsm", [128, 8], F32))
        sqb = pctx.enter_context(nc.sbuf_tensor("sqb", [128, 1024], F32))
        sc1 = pctx.enter_context(nc.sbuf_tensor("sc1", [1, 8], F32))
        negb = pctx.enter_context(nc.sbuf_tensor("negb", [128, 8], F32))
        subctx = ExitStack()
        gkv = [subctx.enter_context(nc.sbuf_tensor(f"gkv{i}", [128, 1024], BF16)) for i in range(2)]
        nbuf = subctx.enter_context(nc.sbuf_tensor("nbuf", [128, 1024], BF16))
        ctxT = subctx.enter_context(nc.sbuf_tensor("ctxT", [128, 8 * 2048], BF16))
        xqT = subctx.enter_context(nc.sbuf_tensor("xqT", [128, 8 * 1024], BF16))
        ps_tr = pctx.enter_context(nc.psum_tensor("p3_tr", [128, 128], BF16))
        ps_big = pctx.enter_context(nc.psum_tensor("p3_big", [128, 512], F32))
        ps_st = [pctx.enter_context(nc.psum_tensor(f"p3_st{i}", [128, 512], F32)) for i in range(2)]
        ps_av = [pctx.enter_context(nc.psum_tensor(f"p3_av{i}", [65, 512], F32)) for i in range(2)]
        ps_z = [pctx.enter_context(nc.psum_tensor(f"p3_z{i}", [64, 512], F32)) for i in range(2)]

        def hcasts(gp):
            gp.dma_start(out=wqh_bf[:].rearrange("p (c n) -> p c n", c=8), in_=wq_h.rearrange("(c p) n -> p c n", p=128)).then_inc(sems["dpl"], 16)
            gp.dma_start(out=wkvhT_bf[:].rearrange("p (c n) -> p c n", c=8), in_=wkv_hT.rearrange("(c p) n -> p c n", p=128)).then_inc(sems["dpl"], 16)
            return gp.dma_start(out=woh_bf[:].rearrange("p (c n) -> p c n", c=8), in_=wo_h.rearrange("c p n -> p c n"))
        P.op("gpsimd", hcasts, incs=[("dpl", 16)])
        cnt["dpl"] += 16 * 2
        bcast_row(gam_h, gamH128, ps_st[0][0:128, 0:512], ps_st[1][0:128, 0:512], rowbuf3)
        P.op("sync", lambda sy: sy.dma_start(out=idxq_sb[:], in_=idx_dram[0][0:1024, :].rearrange("(p j) one -> p (j one)", p=128)),
             waits=[("dpl", None)], incs=[("dsy", 16)])
        P.op("sync", lambda sy: sy.dma_start(out=idxkv_sb[:], in_=idx_dram[1][0:2048, :].rearrange("(p j) one -> p (j one)", p=128)),
             incs=[("dsy", 16)])
        for tt in range(8):
            P.op("gpsimd", lambda gp, tt=tt: gp.indirect_dma_start(out=ownrow_sb[:, tt:tt + 1], out_offset=None,
                 in_=ownmap, in_offset=IndirectOffsetOnAxis(idxq_sb[:, tt:tt + 1], 0)),
                 waits=[("dsy", None)], incs=[("dpl", 16)])
        for h in range(8):
            P.op("vector", lambda v, h=h: v.memset(qTpad[h][:], 0.0), incs=[("dve", 1)])


        def rms_tile(idx_sb, col, dstT, dst_stride, jt, nm):
            j = jt % 2
            P.op("gpsimd", lambda gp: gp.indirect_dma_start(out=gkv[j][:], out_offset=None, in_=xnk,
                 in_offset=IndirectOffsetOnAxis(idx_sb[:, col:col + 1], 0)),
                 waits=[("dsy", None), ("act", None), ("dve", None), ("pool", None)], incs=[("dpl", 16)])
            P.op("scalar", lambda sc: sc.activation(out=sqb[:], in_=gkv[j][:], func=AF.Square, accum_out=hsm[:, 0:1]),
                 waits=[("dpl", None)], incs=[("act", 1)])
            P.op("scalar", lambda sc: sc.sqrt(out=hsm[:, 1:2], in_=hsm[:, 0:1]),
                 waits=[("act", None)], incs=[("act", 1)])
            P.op("vector", lambda v: v.tensor_scalar(out=hsm[:, 2:3], in0=hsm[:, 1:2], scalar1=1e-12, scalar2=None, op0=ALU.max),
                 waits=[("act", None)], incs=[("dve", 1)])
            P.op("vector", lambda v: v.reciprocal(out=hsm[:, 3:4], in_=hsm[:, 2:3]),
                 waits=[("dve", None)], incs=[("dve", 1)])
            P.op("vector", lambda v: v.tensor_scalar(out=hsm[:, 4:5], in0=hsm[:, 3:4], scalar1=32.0, scalar2=None, op0=ALU.mult),
                 waits=[("dve", None)], incs=[("dve", 1)])
            P.op("vector", lambda v: v.tensor_scalar(out=nbuf[:], in0=gkv[j][:], scalar1=hsm[:, 4:5], scalar2=None, op0=ALU.mult),
                 waits=[("dve", None), ("pe", None)], incs=[("dve", 1)])
            P.op("gpsimd", lambda gp: gp.tensor_mul(out=nbuf[:], in0=nbuf[:], in1=gamH128[:]),
                 waits=[("dve", None)], incs=[("pool", 1)])
            for c in range(8):
                P.op("tensor", lambda te, c=c: te.transpose(ps_tr[:], in_=nbuf[:, c * 128:(c + 1) * 128], identity=ident_bf[:]),
                     waits=[("pool", None), ("act", None)], incs=[("pe", 1)])
                P.op("scalar", lambda sc, c=c: sc.copy(out=dstT[:, c * dst_stride + jt * 128: c * dst_stride + jt * 128 + 128], in_=ps_tr[:]),
                     waits=[("pe", None)], incs=[("act", 1)])
        for jt in range(16):
            rms_tile(idxkv_sb, jt, ctxT, 2048, jt, "kv")
        for jt in range(8):
            rms_tile(idxq_sb, jt, xqT, 1024, jt, "q")

        if debug >= 2:
            P.op("gpsimd", lambda gp: gp.dma_start(out=dbg["h_nbuf"], in_=ctxT[:, 4 * 2048: 4 * 2048 + 1024]), waits=[("act", None)], incs=[("dpl", 16)])
            P.op("gpsimd", lambda gp: gp.dma_start(out=dbg["h_kvT"], in_=ctxT[:, 2048:4096]), waits=[("act", None)], incs=[("dpl", 16)])
        P.op("tensor", lambda te: te.nop(), waits=[("dve", None), ("pool", None)], incs=[])
        for h in range(8):
            for half in range(2):
                def mmq(te, h=h, half=half):
                    ins = None
                    for d in range(8):
                        ins = te.matmul(ps_big[0:64, :], lhsT=wqh_bf[:, d * 512 + h * 64: d * 512 + h * 64 + 64],
                                        rhs=xqT[:, d * 1024 + half * 512: d * 1024 + (half + 1) * 512],
                                        start=(d == 0), stop=(d == 7))
                    return ins
                P.op("tensor", mmq, waits=[("act", None)], incs=[("pe", 1)])
                P.op("scalar", lambda sc, h=h, half=half: sc.activation(out=qTpad[h][0:64, half * 512:(half + 1) * 512], in_=ps_big[0:64, :], func=AF.Identity, scale=SCH),
                     waits=[("pe", None), ("dve", None)], incs=[("act", 1)])
        for h in range(8):
            for i4 in range(4):
                def mmkv(te, h=h, i4=i4):
                    ins = None
                    for d in range(8):
                        ins = te.matmul(ps_big[:], lhsT=wkvhT_bf[:, d * 1024 + h * 128: d * 1024 + h * 128 + 128],
                                        rhs=ctxT[:, d * 2048 + i4 * 512: d * 2048 + (i4 + 1) * 512],
                                        start=(d == 0), stop=(d == 7))
                    return ins
                P.op("tensor", mmkv, waits=[("act", None)], incs=[("pe", 1)])
                P.op("scalar", lambda sc, h=h, i4=i4: sc.copy(out=kvT[:, h * 2048 + i4 * 512: h * 2048 + (i4 + 1) * 512], in_=ps_big[:]),
                     waits=[("pe", None)], incs=[("act", 1)])
        if globals().get("P3STAGE", 99) <= 1:
            if debug >= 2:
                P.op("gpsimd", lambda gp: gp.dma_start(out=dbg["h_avf"].rearrange("a b -> (a) (b)")[0:64, 0:1024], in_=kvT[64:128, 0:1024]), waits=[("act", None)], incs=[("dpl", 16)])
            P.flush("p3cut")
            subctx.close()
            pctx.close()
            return
        subctx.close()
        p3bctx = ExitStack()
        kmrow = p3bctx.enter_context(nc.sbuf_tensor("kmrow", [1, 2048], F32))
        ksq = p3bctx.enter_context(nc.sbuf_tensor("ksq", [64, 2048], F32))
        qn2 = p3bctx.enter_context(nc.sbuf_tensor("qn2", [1, 1024], F32))
        vext = p3bctx.enter_context(nc.sbuf_tensor("vext", [128, 8 * 16 * 65], BF16))
        vtmp = p3bctx.enter_context(nc.sbuf_tensor("vtmp", [64, 2048], BF16))
        wbuf = [p3bctx.enter_context(nc.sbuf_tensor(f"wbuf{i}", [128, 1024], BF16)) for i in range(2)]
        avf = p3bctx.enter_context(nc.sbuf_tensor("avf", [65, 1024], F32))
        zrowb = p3bctx.enter_context(nc.sbuf_tensor("zrowb", [1, 1024], F32))
        outT = [p3bctx.enter_context(nc.sbuf_tensor(f"outT{h}", [64, 1024], BF16)) for h in range(8)]
        hv = p3bctx.enter_context(nc.sbuf_tensor("hv", [128, 1024], F32))
        P.op("gpsimd", lambda gp: gp.memset(vext[:], 1.0), waits=[("pe", None), ("act", None)], incs=[("pool", 1)])
        P.op("tensor", lambda te: te.nop(), waits=[("pool", None)], incs=[])
        for h in range(8):
            P.op("sync", lambda sy, h=h: sy.dma_start(out=vbounce, in_=kvT[64:128, h * 2048:(h + 1) * 2048]),
                 waits=[("act", None), ("pe", None), ("dsy", None)], incs=[("dsy", 16)])
            P.op("sync", lambda sy, h=h: sy.dma_start(out=vtmp[0:64, :], in_=vbounce),
                 waits=[("dsy", None)], incs=[("dsy", 16)])
            for jt in range(16):
                P.op("tensor", lambda te, h=h, jt=jt: te.transpose(ps_tr[:, 0:64], in_=vtmp[0:64, jt * 128: jt * 128 + 128], identity=ident_bf[0:64, 0:64]),
                     waits=[("act", None), ("dsy", None)], incs=[("pe", 1)])
                P.op("scalar", lambda sc, h=h, jt=jt: sc.copy(out=vext[:, (h * 16 + jt) * 65:(h * 16 + jt) * 65 + 64], in_=ps_tr[:, 0:64]),
                     waits=[("pe", None)], incs=[("act", 1)])
        if debug >= 2:
            P.op("gpsimd", lambda gp: gp.dma_start(out=dbg["h_qT"], in_=wkvhT_bf[:, 0:1024]), waits=[("act", None)], incs=[("dpl", 16)])
            P.op("gpsimd", lambda gp: gp.dma_start(out=dbg["h_vext"], in_=vext[:, 0:130]), waits=[("act", None)], incs=[("dpl", 16)])
        if globals().get("P3STAGE", 99) <= 1.2:
            if debug >= 2:
                P.op("gpsimd", lambda gp: gp.dma_start(out=dbg["h_kvT"][:, 0:1024], in_=kvT[:, 0:1024]), waits=[("act", None), ("dve", None), ("pool", None), ("pe", None)], incs=[("dpl", 16)])
                P.op("gpsimd", lambda gp: gp.dma_start(out=dbg["h_vext"], in_=vext[:, 0:130]), incs=[("dpl", 16)])
            P.flush("p3cut12")
            p3bctx.close()
            pctx.close()
            return
        for h in range(8):
            P.op("scalar", lambda sc, h=h: sc.activation(out=ksq[:], in_=kvT[0:64, h * 2048:(h + 1) * 2048], func=AF.Square),
                 waits=[("act", None)], incs=[("act", 1)])
            P.op("gpsimd", lambda gp: gp.tensor_reduce(out=kmrow[:], in_=ksq[:], axis=AX.C, op=ALU.add),
                 waits=[("act", None), ("pool", None)], incs=[("pool", 1)])
            P.op("vector", lambda v: v.tensor_reduce(out=sc1[:, 0:1], in_=kmrow[:], axis=AX.X, op=ALU.max),
                 waits=[("pool", None), ("dve", None)], incs=[("dve", 1)])
            P.op("scalar", lambda sc: sc.sqrt(out=sc1[:, 5:6], in_=sc1[:, 0:1]),
                 waits=[("dve", None)], incs=[("act", 1)])
            P.op("scalar", lambda sc, h=h: sc.activation(out=sqb[:, 0:1024], in_=qTpad[h][:, 0:1024], func=AF.Square),
                 waits=[("act", None)], incs=[("act", 1)])
            P.op("gpsimd", lambda gp: gp.tensor_reduce(out=qn2[:], in_=sqb[:, 0:1024], axis=AX.C, op=ALU.add),
                 waits=[("act", None), ("pool", None)], incs=[("pool", 1)])
            P.op("vector", lambda v: v.tensor_reduce(out=sc1[:, 1:2], in_=qn2[:], axis=AX.X, op=ALU.max),
                 waits=[("pool", None), ("dve", None)], incs=[("dve", 1)])
            P.op("scalar", lambda sc: sc.sqrt(out=sc1[:, 2:3], in_=sc1[:, 1:2]),
                 waits=[("dve", None)], incs=[("act", 1)])
            P.op("vector", lambda v: v.tensor_mul(out=sc1[:, 3:4], in0=sc1[:, 5:6], in1=sc1[:, 2:3]),
                 waits=[("act", None), ("dve", None)], incs=[("dve", 1)])
            P.op("vector", lambda v: v.tensor_scalar(out=sc1[:, 4:5], in0=sc1[:, 3:4], scalar1=-1.0, scalar2=None, op0=ALU.mult),
                 waits=[("dve", None)], incs=[("dve", 1)])
            P.op("tensor", lambda te: te.matmul(ps_z[0][0:128, 0:1], lhsT=ones_row_f[:], rhs=sc1[0:1, 4:5], start=True, stop=True)
                 if False else te.matmul(ps_big[:, 0:1], lhsT=ones_row_f[:], rhs=sc1[0:1, 4:5], start=True, stop=True),
                 waits=[("dve", None), ("act", None)], incs=[("pe", 1)])
            P.op("vector", lambda v, h=h: v.tensor_copy(out=negb[:, h:h + 1], in_=ps_big[:, 0:1]),
                 waits=[("pe", None)], incs=[("dve", 1)])
            P.op("tensor", lambda te: te.nop(), waits=[("dve", None)], incs=[])
        if debug >= 2:
            P.op("gpsimd", lambda gp: gp.dma_start(out=dbg["h_negb"], in_=negb[:]), waits=[("dve", None)], incs=[("dpl", 16)])
        if globals().get("P3STAGE", 99) <= 1.5:
            if debug >= 2:
                P.op("gpsimd", lambda gp: gp.dma_start(out=dbg["h_kvT"][:, 0:1024], in_=kvT[:, 0:1024]), waits=[("act", None), ("dve", None), ("pool", None), ("pe", None)], incs=[("dpl", 16)])
                P.op("gpsimd", lambda gp: gp.dma_start(out=dbg["h_vext"], in_=vext[:, 0:130]), incs=[("dpl", 16)])
                P.op("gpsimd", lambda gp: gp.dma_start(out=dbg["h_negb"], in_=negb[:]), incs=[("dpl", 16)])
            P.flush("p3cut15")
            p3bctx.close()
            pctx.close()
            return
        for h in range(8):
            for jt in range(16):
                def mmst(te, h=h, jt=jt):
                    te.matmul(ps_st[0][:], lhsT=kvT[:, h * 2048 + jt * 128: h * 2048 + jt * 128 + 128], rhs=qTpad[h][:, 0:512], start=True, stop=True)
                    return te.matmul(ps_st[1][:], lhsT=kvT[:, h * 2048 + jt * 128: h * 2048 + jt * 128 + 128], rhs=qTpad[h][:, 512:1024], start=True, stop=True)
                P.op("tensor", mmst, waits=[("act", None)], incs=[("pe", 1)])
                jj = jt % 2
                P.op("scalar", lambda sc, h=h, jj=jj: sc.activation(out=wbuf[jj][:, 0:512], in_=ps_st[0][:], func=AF.Exp, bias=negb[:, h:h + 1]),
                     waits=[("pe", None), ("dve", None)], incs=[("act", 1)])
                P.op("scalar", lambda sc, h=h, jj=jj: sc.activation(out=wbuf[jj][:, 512:1024], in_=ps_st[1][:], func=AF.Exp, bias=negb[:, h:h + 1]),
                     incs=[("act", 1)])
                def mmav(te, h=h, jt=jt, jj=jj):
                    te.matmul(ps_av[0][:], lhsT=vext[:, (h * 16 + jt) * 65:(h * 16 + jt + 1) * 65], rhs=wbuf[jj][:, 0:512], start=(jt == 0), stop=(jt == 15))
                    return te.matmul(ps_av[1][:], lhsT=vext[:, (h * 16 + jt) * 65:(h * 16 + jt + 1) * 65], rhs=wbuf[jj][:, 512:1024], start=(jt == 0), stop=(jt == 15))
                P.op("tensor", mmav, waits=[("act", None)], incs=[("pe", 1)])
            P.op("scalar", lambda sc: sc.copy(out=avf[:, 0:512], in_=ps_av[0][:]), waits=[("pe", None)], incs=[("act", 1)])
            P.op("scalar", lambda sc: sc.copy(out=avf[:, 512:1024], in_=ps_av[1][:]), incs=[("act", 1)])
            P.op("vector", lambda v: v.tensor_scalar(out=zrowb[:], in0=avf[64:65, :], scalar1=1e-35, scalar2=None, op0=ALU.max),
                 waits=[("act", None), ("dve", None)], incs=[("dve", 1)])
            P.op("vector", lambda v: v.reciprocal(out=zrowb[:], in_=zrowb[:]),
                 waits=[("dve", None)], incs=[("dve", 1)])
            def mmz(te):
                te.matmul(ps_z[0][:], lhsT=ones_row_f[0:1, 0:64], rhs=zrowb[0:1, 0:512], start=True, stop=True)
                return te.matmul(ps_z[1][:], lhsT=ones_row_f[0:1, 0:64], rhs=zrowb[0:1, 512:1024], start=True, stop=True)
            P.op("tensor", mmz, waits=[("dve", None)], incs=[("pe", 1)])
            P.op("vector", lambda v, h=h: v.tensor_mul(out=outT[h][:, 0:512], in0=avf[0:64, 0:512], in1=ps_z[0][:]),
                 waits=[("pe", None)], incs=[("dve", 1)])
            P.op("vector", lambda v, h=h: v.tensor_mul(out=outT[h][:, 512:1024], in0=avf[0:64, 512:1024], in1=ps_z[1][:]),
                 incs=[("dve", 1)])
            P.op("tensor", lambda te: te.nop(), waits=[("dve", None)], incs=[])
        if globals().get("P3STAGE", 99) <= 2:
            if debug >= 2:
                P.op("gpsimd", lambda gp: gp.dma_start(out=dbg["h_avf"], in_=avf[:]), waits=[("dve", None)], incs=[("dpl", 16)])
                P.op("gpsimd", lambda gp: gp.dma_start(out=dbg["h_negb"], in_=negb[:]), incs=[("dpl", 16)])
                P.op("gpsimd", lambda gp: gp.dma_start(out=dbg["h_qT"], in_=qTpad[7][:]), waits=[("act", None)], incs=[("dpl", 16)])
                P.op("gpsimd", lambda gp: gp.dma_start(out=dbg["h_vext"], in_=vext[:, 0:130]), incs=[("dpl", 16)])
                P.op("gpsimd", lambda gp: gp.dma_start(out=dbg["h_kvT"][:, 0:1024], in_=kvT[:, 0:1024]), incs=[("dpl", 16)])
            P.flush("p3cut2")
            p3bctx.close()
            pctx.close()
            return
        for tt in range(8):
            for half in range(2):
                def mmwo(te, tt=tt, half=half):
                    ins = None
                    for h in range(8):
                        ins = te.matmul(ps_st[0][:] if half == 0 else ps_big[:],
                                        lhsT=outT[h][:, tt * 128:(tt + 1) * 128],
                                        rhs=woh_bf[0:64, h * 1024 + half * 512: h * 1024 + (half + 1) * 512],
                                        start=(h == 0), stop=(h == 7))
                    return ins
                P.op("tensor", mmwo, waits=[("dve", None), ("dsc", None)], incs=[("pe", 1)])
            P.op("vector", lambda v: v.tensor_copy(out=hv[:, 0:512], in_=ps_st[0][:]), waits=[("pe", None)], incs=[("dve", 1)])
            P.op("vector", lambda v: v.tensor_copy(out=hv[:, 512:1024], in_=ps_big[:]), incs=[("dve", 1)])
            P.op("scalar", lambda sc, tt=tt: sc.dma_start(out=heavy_dram[tt * 128:(tt + 1) * 128, :], in_=hv[:]),
                 waits=[("dve", None)], incs=[("dsc", 16)])
            P.op("tensor", lambda te: te.nop(), waits=[("dve", None)], incs=[])
        if debug >= 1:
            P.op("sync", lambda sy: sy.dma_start(out=dbg["heavy"], in_=heavy_dram), waits=[("dsc", None)], incs=[("dsy", 16)])
        P.flush("p3")
        p3bctx.close()
        pctx.close()

    def p19_p15_p4(run15=True, run4=True):
        pctx = ExitStack()
        xnT = pctx.enter_context(nc.sbuf_tensor("xnT", [128, 8 * 8192], BF16))
        for c in range(8):
            P.op("sync", lambda sy, c=c: sy.dma_start(out=xnT[:, c * 8192:(c + 1) * 8192], in_=xnk[:, c * 128:(c + 1) * 128], transpose=True),
                 waits=[("dsc", None)], incs=[("dsy", 16)])
        P.flush("p19")
        if run15:
            p15ctx = ExitStack()
            ps_kq = [p15ctx.enter_context(nc.psum_tensor(f"p15_kq{i}", [128, 64], F32)) for i in range(2)]
            kqb = [p15ctx.enter_context(nc.sbuf_tensor(f"kqb{i}", [128, 64], BF16)) for i in range(2)]
            for i in range(NT):
                j = i % 2
                def mmkq(te, i=i, j=j):
                    ins = None
                    for d in range(8):
                        ins = te.matmul(ps_kq[j][:], lhsT=xnT[:, d * 8192 + i * 128: d * 8192 + i * 128 + 128],
                                        rhs=wqkT_bf[:, d * 64:(d + 1) * 64], start=(d == 0), stop=(d == 7))
                    return ins
                P.op("tensor", mmkq, waits=[("dsy", None), ("act", None)], incs=[("pe", 1)])
                P.op("scalar", lambda sc, j=j: sc.copy(out=kqb[j][:], in_=ps_kq[j][:]), waits=[("pe", None), ("dsc", None)], incs=[("act", 1)])
                P.op("scalar", lambda sc, i=i, j=j: sc.dma_start(out=xnk[i * 128:(i + 1) * 128, 1024:1056], in_=kqb[j][:, 0:32]),
                     waits=[("act", None)], incs=[("dsc", 16)])
                P.op("scalar", lambda sc, i=i, j=j: sc.dma_start(out=q_dram[i * 128:(i + 1) * 128, :], in_=kqb[j][:, 32:64]),
                     incs=[("dsc", 16)])
            if debug >= 1:
                P.op("sync", lambda sy: sy.dma_start(out=dbg["kq"], in_=xnk[:, 1024:1056]), waits=[("dsc", None)], incs=[("dsy", 16)]) if False else None
            P.flush("p15")
            p15ctx.close()
        if not run4:
            pctx.close()
            return
        p4ctx = ExitStack()
        simrow = p4ctx.enter_context(nc.sbuf_tensor("simrow", [128, 8192], F32))
        scr512 = p4ctx.enter_context(nc.sbuf_tensor("scr512", [128, 512], F32))
        gq = p4ctx.enter_context(nc.sbuf_tensor("gq", [128, 1056], BF16))
        qbT = p4ctx.enter_context(nc.sbuf_tensor("qbT", [128, 8 * 128], BF16))
        q_sb = p4ctx.enter_context(nc.sbuf_tensor("q_sb", [128, 32], BF16))
        gath = [p4ctx.enter_context(nc.sbuf_tensor(f"gath{j}", [128, 1056], BF16)) for j in range(8)]
        vals8 = p4ctx.enter_context(nc.sbuf_tensor("vals8", [128, 8], F32))
        idx8 = p4ctx.enter_context(nc.sbuf_tensor("idx8", [128, 8], U32))
        vm8 = p4ctx.enter_context(nc.sbuf_tensor("vm8", [128, 8], U8))
        sc8 = p4ctx.enter_context(nc.sbuf_tensor("p4sc8", [128, 8], F32))
        scw = p4ctx.enter_context(nc.sbuf_tensor("p4scw", [128, 8], F32))
        sm8 = p4ctx.enter_context(nc.sbuf_tensor("p4sm8", [128, 8], F32))
        nst = p4ctx.enter_context(nc.sbuf_tensor("p4nst", [128, 4], F32))
        pbuf = [p4ctx.enter_context(nc.sbuf_tensor(f"pbuf{i}", [128, 1024], F32)) for i in range(2)]
        msgb = p4ctx.enter_context(nc.sbuf_tensor("msgb", [128, 1024], F32))
        rtmp = pbuf[0]
        f01 = p4ctx.enter_context(nc.sbuf_tensor("f01", [128, 1], F32))
        ps_tr4 = p4ctx.enter_context(nc.psum_tensor("p4_tr", [128, 128], BF16))
        ps_sim = [p4ctx.enter_context(nc.psum_tensor(f"p4_sim{i}", [128, 512], F32)) for i in range(2)]

        for s in range(NS):
            W = wslot(s)
            nch = W // 512
            P.op("gpsimd", lambda gp, s=s: gp.indirect_dma_start(out=gq[:], out_offset=None, in_=xnk,
                 in_offset=IndirectOffsetOnAxis(rowidu_sb[:, s:s + 1], 0)),
                 waits=[("dsy", None), ("act", None), ("dve", None), ("pe", None), ("pool", None)], incs=[("dpl", 16)])
            P.op("gpsimd", lambda gp, s=s: gp.indirect_dma_start(out=q_sb[:], out_offset=None, in_=q_dram,
                 in_offset=IndirectOffsetOnAxis(rowidu_sb[:, s:s + 1], 0)),
                 incs=[("dpl", 16)])
            for c in range(8):
                P.op("tensor", lambda te, c=c: te.transpose(ps_tr4[:], in_=gq[:, c * 128:(c + 1) * 128], identity=ident_bf[:]),
                     waits=[("dpl", None)] if c == 0 else [], incs=[("pe", 1)])
                P.op("scalar", lambda sc, c=c: sc.copy(out=qbT[:, c * 128:(c + 1) * 128], in_=ps_tr4[:]),
                     waits=[("pe", None)], incs=[("act", 1)])
                P.op("tensor", lambda te: te.nop(), waits=[("act", None)], incs=[])
            for c in range(nch):
                def mmsim(te, s=s, c=c):
                    ins = None
                    for d in range(8):
                        ins = te.matmul(ps_sim[c % 2][:], lhsT=qbT[:, d * 128:(d + 1) * 128],
                                        rhs=xnT[:, d * 8192 + c * 512: d * 8192 + (c + 1) * 512],
                                        start=(d == 0), stop=(d == 7))
                    return ins
                P.op("tensor", mmsim, waits=[("act", None)] if c < 2 else [("act", None)], incs=[("pe", 1)])
                P.op("scalar", lambda sc, c=c: sc.copy(out=simrow[:, c * 512:(c + 1) * 512], in_=ps_sim[c % 2][:]),
                     waits=[("pe", None)], incs=[("act", 1)])
            P.op("vector", lambda v, s=s: v.tensor_scalar(out=scr512[:], in0=iota512[:], scalar1=bound_sb[:, s:s + 1], scalar2=None, op0=ALU.is_lt),
                 waits=[("dve", None)], incs=[("dve", 1)])
            P.op("vector", lambda v: v.tensor_scalar(out=scr512[:], in0=scr512[:], scalar1=1.0e30, scalar2=-1.0e30, op0=ALU.mult, op1=ALU.add),
                 waits=[("dve", None)], incs=[("dve", 1)])
            P.op("vector", lambda v, nch=nch: v.tensor_add(out=simrow[:, (nch - 1) * 512: nch * 512], in0=simrow[:, (nch - 1) * 512: nch * 512], in1=scr512[:]),
                 waits=[("dve", None), ("act", None)], incs=[("dve", 1)])
            P.op("vector", lambda v, W=W: v.max(out=vals8[:], in_=simrow[:, 0:W]),
                 waits=[("act", None), ("dpl", None)], incs=[("dve", 1)])
            P.op("vector", lambda v, W=W: v.max_index(out=idx8[:], in_max=vals8[:], in_values=simrow[:, 0:W]),
                 waits=[("dve", None)], incs=[("dve", 1)])
            P.op("vector", lambda v: v.tensor_scalar(out=vm8[:], in0=vals8[:], scalar1=-1.0e29, scalar2=None, op0=ALU.is_gt),
                 waits=[("dve", None)], incs=[("dve", 1)])
            dve_idx_done = cnt["dve"]
            for jg in range(8):
                P.op("gpsimd", lambda gp, jg=jg: gp.indirect_dma_start(out=gath[jg][:], out_offset=None, in_=xnk,
                     in_offset=IndirectOffsetOnAxis(idx8[:, jg:jg + 1], 0)),
                     waits=[("dve", dve_idx_done), ("pool", None)] if jg == 0 else [], incs=[("dpl", 16)])
            for jg in range(8):
                P.op("vector", lambda v, jg=jg: v.tensor_mul(out=scr512[:, jg * 32:(jg + 1) * 32], in0=q_sb[:], in1=gath[jg][:, 1024:1056]),
                     waits=[("dpl", None)] if jg == 0 else [], incs=[("dve", 1)])
            P.op("vector", lambda v: v.tensor_reduce(out=sc8[:].rearrange("p (j one) -> p j one", j=8), in_=scr512[:, 0:256].rearrange("p (j d) -> p j d", j=8), axis=AX.X, op=ALU.add),
                 waits=[("dve", None)], incs=[("dve", 1)])
            P.op("vector", lambda v: v.tensor_scalar(out=sc8[:], in0=sc8[:], scalar1=SC32, scalar2=None, op0=ALU.mult),
                 waits=[("dve", None)], incs=[("dve", 1)])
            P.op("vector", lambda v: v.select(out=sm8[:], mask=vm8[:], on_true=sc8[:], on_false=neg9_8[:]),
                 waits=[("dve", None)], incs=[("dve", 1)])
            P.op("vector", lambda v: v.reduce_max(out=nst[:, 0:1], in_=sm8[:], axis=AX.X, negate=True),
                 waits=[("dve", None)], incs=[("dve", 1)])
            P.op("scalar", lambda sc: sc.activation(out=scw[:], in_=sm8[:], func=AF.Exp, bias=nst[:, 0:1]),
                 waits=[("dve", None)], incs=[("act", 1)])
            P.op("vector", lambda v: v.reduce_sum(out=nst[:, 1:2], in_=scw[:], axis=AX.X),
                 waits=[("act", None)], incs=[("dve", 1)])
            P.op("vector", lambda v: v.tensor_scalar(out=nst[:, 1:2], in0=nst[:, 1:2], scalar1=1e-30, scalar2=None, op0=ALU.max),
                 waits=[("dve", None)], incs=[("dve", 1)])
            P.op("vector", lambda v: v.reciprocal(out=nst[:, 2:3], in_=nst[:, 1:2]),
                 waits=[("dve", None)], incs=[("dve", 1)])
            P.op("vector", lambda v: v.tensor_scalar(out=scw[:], in0=scw[:], scalar1=nst[:, 2:3], scalar2=None, op0=ALU.mult),
                 waits=[("dve", None)], incs=[("dve", 1)])
            P.op("vector", lambda v: v.tensor_scalar(out=pbuf[0][:], in0=gath[0][:, 0:1024], scalar1=scw[:, 0:1], scalar2=None, op0=ALU.mult),
                 waits=[("dve", None), ("dsc", None), ("pool", None)], incs=[("dve", 1)])
            P.op("scalar", lambda sc: sc.activation(out=pbuf[1][:], in_=gath[1][:, 0:1024], func=AF.Identity, scale=scw[:, 1:2]),
                 waits=[("dve", None), ("dsc", None), ("pool", None)], incs=[("act", 1)])
            P.op("gpsimd", lambda gp: gp.tensor_add(out=msgb[:], in0=pbuf[0][:], in1=pbuf[1][:]),
                 waits=[("dve", None), ("act", None)], incs=[("pool", 1)])
            for jg in range(2, 8):
                e2 = "vector" if jg % 2 == 0 else "scalar"
                bi = jg % 2
                if e2 == "vector":
                    P.op("vector", lambda v, jg=jg, bi=bi: v.tensor_scalar(out=pbuf[bi][:], in0=gath[jg][:, 0:1024], scalar1=scw[:, jg:jg + 1], scalar2=None, op0=ALU.mult),
                         waits=[("pool", None)], incs=[("dve", 1)])
                else:
                    P.op("scalar", lambda sc, jg=jg, bi=bi: sc.activation(out=pbuf[bi][:], in_=gath[jg][:, 0:1024], func=AF.Identity, scale=scw[:, jg:jg + 1]),
                         waits=[("pool", None)], incs=[("act", 1)])
                P.op("gpsimd", lambda gp, bi=bi: gp.tensor_add(out=msgb[:], in0=msgb[:], in1=pbuf[bi][:]),
                     waits=[("dve", None), ("act", None), ("pool", None)], incs=[("pool", 1)])
            if s == 0:
                P.op("vector", lambda v: v.tensor_scalar(out=f01[:], in0=rowid_sb[:, 0:1], scalar1=0.5, scalar2=None, op0=ALU.is_lt),
                     incs=[("dve", 1)])
                P.op("vector", lambda v: v.tensor_sub(out=rtmp[:], in0=mean128[:], in1=msgb[:]),
                     waits=[("dve", None), ("pool", None)], incs=[("dve", 1)])
                P.op("vector", lambda v: v.scalar_tensor_tensor(out=msgb[:], in0=rtmp[:], scalar=f01[:], in1=msgb[:], op0=ALU.mult, op1=ALU.add),
                     waits=[("dve", None)], incs=[("dve", 1)])
            P.op("scalar", lambda sc, s=s: sc.dma_start(out=msg_dram[s * 128:(s + 1) * 128, :], in_=msgb[:]),
                 waits=[("dve", None), ("pool", None)], incs=[("dsc", 16)])
        if debug >= 1:
            P.op("sync", lambda sy: sy.dma_start(out=dbg["msg"], in_=msg_dram), waits=[("dsc", None)], incs=[("dsy", 16)])
        P.flush("p4")
        p4ctx.close()
        pctx.close()

    def p5():
        pctx = ExitStack()
        wql_bf = pctx.enter_context(nc.sbuf_tensor("wql_bf", [128, 8 * 512], BF16))
        wkvl_bf = pctx.enter_context(nc.sbuf_tensor("wkvl_bf", [128, 8 * 1024], BF16))
        wol_bf = pctx.enter_context(nc.sbuf_tensor("wol_bf", [64, 8 * 1024], BF16))
        gain128 = pctx.enter_context(nc.sbuf_tensor("gain128", [128, D], F32))
        bias128 = pctx.enter_context(nc.sbuf_tensor("bias128", [128, D], F32))
        gamL128 = pctx.enter_context(nc.sbuf_tensor("gamL128", [128, D], BF16))
        rowbuf5 = pctx.enter_context(nc.sbuf_tensor("rowbuf5", [1, D], F32))

        xb = [pctx.enter_context(nc.sbuf_tensor(f"p5xb{i}", [128, D], F32)) for i in range(2)]
        xc = pctx.enter_context(nc.sbuf_tensor("p5xc", [128, D], F32))
        msb = [pctx.enter_context(nc.sbuf_tensor(f"p5ms{i}", [128, D], F32)) for i in range(2)]
        sq5 = pctx.enter_context(nc.sbuf_tensor("p5sq", [128, D], F32))
        tb = pctx.enter_context(nc.sbuf_tensor("p5tb", [128, D], BF16))
        xn2 = pctx.enter_context(nc.sbuf_tensor("p5xn2", [128, D], BF16))
        x2T = pctx.enter_context(nc.sbuf_tensor("p5x2T", [128, 8 * 128], BF16))
        vtok = pctx.enter_context(nc.sbuf_tensor("p5v", [128, 512], BF16))
        qTp = [pctx.enter_context(nc.sbuf_tensor(f"p5qT{h}", [128, 128], BF16)) for h in range(8)]
        kTp = [pctx.enter_context(nc.sbuf_tensor(f"p5kT{h}", [128, 128], BF16)) for h in range(8)]
        wf = pctx.enter_context(nc.sbuf_tensor("p5wf", [128, 128], F32))
        wb = pctx.enter_context(nc.sbuf_tensor("p5wb", [128, 128], BF16))
        wtb = pctx.enter_context(nc.sbuf_tensor("p5wtb", [128, 128], BF16))
        avb = [pctx.enter_context(nc.sbuf_tensor(f"p5av{h}", [64, 128], BF16)) for h in range(8)]
        st5 = pctx.enter_context(nc.sbuf_tensor("p5st", [128, 16], F32))
        e1 = pctx.enter_context(nc.sbuf_tensor("p5e1", [128, D], F32))
        e2 = pctx.enter_context(nc.sbuf_tensor("p5e2", [128, D], F32))
        e3 = pctx.enter_context(nc.sbuf_tensor("p5e3", [128, D], F32))
        e4 = pctx.enter_context(nc.sbuf_tensor("p5e4", [128, D], F32))
        ob = pctx.enter_context(nc.sbuf_tensor("p5ob", [128, D], F32))
        ps_tp = pctx.enter_context(nc.psum_tensor("p5_tp", [128, 128], BF16))
        ps_v = pctx.enter_context(nc.psum_tensor("p5_v", [128, 512], F32))
        ps_qa = pctx.enter_context(nc.psum_tensor("p5_qa", [128, 128], F32))
        ps_qb = pctx.enter_context(nc.psum_tensor("p5_qb", [128, 128], F32))
        ps_st5 = pctx.enter_context(nc.psum_tensor("p5_st", [128, 128], F32))
        ps_av5 = pctx.enter_context(nc.psum_tensor("p5_av", [64, 128], F32))
        ps_l = [pctx.enter_context(nc.psum_tensor(f"p5_l{i}", [128, 512], F32)) for i in range(2)]
        def lcasts(gp):
            gp.dma_start(out=wql_bf[:].rearrange("p (c n) -> p c n", c=8), in_=wq_l.rearrange("(c p) n -> p c n", p=128)).then_inc(sems["dpl"], 16)
            gp.dma_start(out=wkvl_bf[:].rearrange("p (c n) -> p c n", c=8), in_=wkv_l.rearrange("(c p) n -> p c n", p=128)).then_inc(sems["dpl"], 16)
            return gp.dma_start(out=wol_bf[:].rearrange("p (c n) -> p c n", c=8), in_=wo_l.rearrange("c p n -> p c n"))
        P.op("gpsimd", lcasts, incs=[("dpl", 16)])
        cnt["dpl"] += 16 * 2
        bcast_row(gain_in, gain128, ps_l[0], ps_l[1], rowbuf5)
        bcast_row(bias_in, bias128, ps_l[0], ps_l[1], rowbuf5)
        bcast_row(gam_l, gamL128, ps_l[0], ps_l[1], rowbuf5)

        for h in range(8):
            P.op("vector", lambda v, h=h: v.memset(qTp[h][:], 0.0), incs=[("dve", 1)])
            P.op("vector", lambda v, h=h: v.memset(kTp[h][:], 0.0), incs=[("dve", 1)])

        for s in range(NS):
            j = s % 2
            P.op("sync", lambda sy, s=s, j=j: sy.dma_start(out=xb[j][:], in_=x_own[s * 128:(s + 1) * 128, :]),
                 waits=[("dve", max(0, cnt["dve"] - 4))], incs=[("dsy", 16)])
            P.op("sync", lambda sy, s=s, j=j: sy.dma_start(out=msb[j][:], in_=msg_dram[s * 128:(s + 1) * 128, :]),
                 waits=[("dsc", None)], incs=[("dsy", 16)])
            P.op("vector", lambda v, j=j: v.reduce_sum(out=st5[:, 0:1], in_=xb[j][:], axis=AX.X, negate=True),
                 waits=[("dsy", None)], incs=[("dve", 1)])
            P.op("vector", lambda v: v.tensor_scalar(out=st5[:, 1:2], in0=st5[:, 0:1], scalar1=1.0 / 1024.0, scalar2=None, op0=ALU.mult),
                 waits=[("dve", None)], incs=[("dve", 1)])
            P.op("scalar", lambda sc, j=j: sc.activation(out=xc[:], in_=xb[j][:], func=AF.Identity, bias=st5[:, 1:2]),
                 waits=[("dve", None)], incs=[("act", 1)])
            P.op("scalar", lambda sc: sc.activation(out=sq5[:], in_=xc[:], func=AF.Square, accum_out=st5[:, 2:3]),
                 waits=[("act", None)], incs=[("act", 1)])
            P.op("vector", lambda v: v.tensor_scalar(out=st5[:, 3:4], in0=st5[:, 2:3], scalar1=1.0 / 1024.0, scalar2=1e-5, op0=ALU.mult, op1=ALU.add),
                 waits=[("act", None)], incs=[("dve", 1)])
            P.op("scalar", lambda sc: sc.sqrt(out=st5[:, 3:4], in_=st5[:, 3:4]),
                 waits=[("dve", None)], incs=[("act", 1)])
            P.op("vector", lambda v: v.reciprocal(out=st5[:, 4:5], in_=st5[:, 3:4]),
                 waits=[("act", None)], incs=[("dve", 1)])
            P.op("vector", lambda v: v.tensor_mul(out=st5[:, 5:6], in0=st5[:, 4:5], in1=st5[:, 4:5]),
                 waits=[("dve", None)], incs=[("dve", 1)])
            P.op("vector", lambda v: v.tensor_mul(out=st5[:, 5:6], in0=st5[:, 5:6], in1=st5[:, 2:3]),
                 waits=[("dve", None)], incs=[("dve", 1)])
            P.op("scalar", lambda sc: sc.sqrt(out=st5[:, 6:7], in_=st5[:, 5:6]),
                 waits=[("dve", None)], incs=[("act", 1)])
            P.op("vector", lambda v: v.tensor_scalar(out=st5[:, 6:7], in0=st5[:, 6:7], scalar1=1e-12, scalar2=None, op0=ALU.max),
                 waits=[("act", None)], incs=[("dve", 1)])
            P.op("vector", lambda v: v.reciprocal(out=st5[:, 7:8], in_=st5[:, 6:7]),
                 waits=[("dve", None)], incs=[("dve", 1)])
            P.op("vector", lambda v: v.tensor_scalar(out=st5[:, 7:8], in0=st5[:, 7:8], scalar1=32.0, scalar2=None, op0=ALU.mult),
                 waits=[("dve", None)], incs=[("dve", 1)])
            P.op("vector", lambda v: v.tensor_mul(out=st5[:, 8:9], in0=st5[:, 7:8], in1=st5[:, 4:5]),
                 waits=[("dve", None)], incs=[("dve", 1)])
            P.op("vector", lambda v: v.tensor_scalar(out=st5[:, 9:10], in0=st5[:, 4:5], scalar1=mixsc[:, 0:1], scalar2=None, op0=ALU.mult),
                 waits=[("dve", None)], incs=[("dve", 1)])
            P.op("scalar", lambda sc: sc.activation(out=tb[:], in_=xc[:], func=AF.Identity, scale=st5[:, 8:9]),
                 waits=[("dve", None)], incs=[("act", 1)])
            P.op("vector", lambda v: v.tensor_mul(out=xn2[:], in0=tb[:], in1=gamL128[:]),
                 waits=[("act", None)], incs=[("dve", 1)])
            for c in range(8):
                P.op("tensor", lambda te, c=c: te.transpose(ps_tp[:], in_=xn2[:, c * 128:(c + 1) * 128], identity=ident_bf[:]),
                     waits=[("dve", None), ("act", None)], incs=[("pe", 1)])
                P.op("scalar", lambda sc, c=c: sc.copy(out=x2T[:, c * 128:(c + 1) * 128], in_=ps_tp[:]),
                     waits=[("pe", None)], incs=[("act", 1)])
                P.op("tensor", lambda te: te.nop(), waits=[("act", None)], incs=[])
            def mmv(te):
                ins = None
                for d in range(8):
                    ins = te.matmul(ps_v[:], lhsT=x2T[:, d * 128:(d + 1) * 128], rhs=wkvl_bf[:, d * 1024 + 512: d * 1024 + 1024], start=(d == 0), stop=(d == 7))
                return ins
            P.op("tensor", mmv, waits=[("act", None)], incs=[("pe", 1)])
            P.op("scalar", lambda sc: sc.copy(out=vtok[:], in_=ps_v[:]), waits=[("pe", None)], incs=[("act", 1)])
            for hc in range(4):
                def mmqk(te, hc=hc):
                    ins = None
                    for d in range(8):
                        te.matmul(ps_qa[:], lhsT=wql_bf[:, d * 512 + hc * 128: d * 512 + hc * 128 + 128], rhs=x2T[:, d * 128:(d + 1) * 128], start=(d == 0), stop=(d == 7))
                        ins = te.matmul(ps_qb[:], lhsT=wkvl_bf[:, d * 1024 + hc * 128: d * 1024 + hc * 128 + 128], rhs=x2T[:, d * 128:(d + 1) * 128], start=(d == 0), stop=(d == 7))
                    return ins
                P.op("tensor", mmqk, waits=[("act", None)], incs=[("pe", 1)])
                P.op("scalar", lambda sc, hc=hc: sc.activation(out=qTp[2 * hc][0:64, :], in_=ps_qa[0:64, :], func=AF.Identity, scale=SCH),
                     waits=[("pe", None)], incs=[("act", 1)])
                P.op("scalar", lambda sc, hc=hc: sc.activation(out=qTp[2 * hc + 1][64:128, :], in_=ps_qa[64:128, :], func=AF.Identity, scale=SCH),
                     incs=[("act", 1)])
                P.op("scalar", lambda sc, hc=hc: sc.copy(out=kTp[2 * hc][0:64, :], in_=ps_qb[0:64, :]),
                     incs=[("act", 1)])
                P.op("scalar", lambda sc, hc=hc: sc.copy(out=kTp[2 * hc + 1][64:128, :], in_=ps_qb[64:128, :]),
                     incs=[("act", 1)])
                P.op("tensor", lambda te: te.nop(), waits=[("act", None)], incs=[])
            for h in range(8):
                P.op("tensor", lambda te, h=h: te.matmul(ps_st5[:], lhsT=qTp[h][:], rhs=kTp[h][:], start=True, stop=True),
                     waits=[("act", None)], incs=[("pe", 1)])
                P.op("vector", lambda v: v.reduce_max(out=st5[:, 10:11], in_=ps_st5[:], axis=AX.X, negate=True),
                     waits=[("pe", None), ("dve", None)], incs=[("dve", 1)])
                P.op("scalar", lambda sc: sc.activation(out=wf[:], in_=ps_st5[:], func=AF.Exp, bias=st5[:, 10:11]),
                     waits=[("dve", None)], incs=[("act", 1)])
                P.op("vector", lambda v: v.reduce_sum(out=st5[:, 11:12], in_=wf[:], axis=AX.X),
                     waits=[("act", None)], incs=[("dve", 1)])
                P.op("vector", lambda v: v.reciprocal(out=st5[:, 12:13], in_=st5[:, 11:12]),
                     waits=[("dve", None)], incs=[("dve", 1)])
                P.op("vector", lambda v: v.tensor_scalar(out=wb[:], in0=wf[:], scalar1=st5[:, 12:13], scalar2=None, op0=ALU.mult),
                     waits=[("dve", None)], incs=[("dve", 1)])
                P.op("tensor", lambda te: te.transpose(ps_tp[:], in_=wb[:], identity=ident_bf[:]),
                     waits=[("dve", None)], incs=[("pe", 1)])
                P.op("scalar", lambda sc: sc.copy(out=wtb[:], in_=ps_tp[:]),
                     waits=[("pe", None)], incs=[("act", 1)])
                P.op("tensor", lambda te, h=h: te.matmul(ps_av5[:], lhsT=vtok[:, h * 64:(h + 1) * 64], rhs=wtb[:], start=True, stop=True),
                     waits=[("act", None)], incs=[("pe", 1)])
                P.op("scalar", lambda sc, h=h: sc.copy(out=avb[h][:], in_=ps_av5[:]),
                     waits=[("pe", None)], incs=[("act", 1)])
                P.op("tensor", lambda te: te.nop(), waits=[("act", None)], incs=[])
            for half in range(2):
                def mmwo5(te, half=half):
                    ins = None
                    for h in range(8):
                        ins = te.matmul(ps_l[half][:], lhsT=avb[h][:], rhs=wol_bf[0:64, h * 1024 + half * 512: h * 1024 + (half + 1) * 512], start=(h == 0), stop=(h == 7))
                    return ins
                P.op("tensor", mmwo5, waits=[("act", None), ("dve", None)], incs=[("pe", 1)])
            P.op("scalar", lambda sc: sc.activation(out=e1[:], in_=xc[:], func=AF.Identity, scale=st5[:, 9:10]),
                 waits=[("dve", None)], incs=[("act", 1)])
            P.op("vector", lambda v, j=j: v.scalar_tensor_tensor(out=e2[:], in0=msb[j][:], scalar=mixsc[:, 1:2], in1=e1[:], op0=ALU.mult, op1=ALU.add),
                 waits=[("act", None)], incs=[("dve", 1)])
            P.op("gpsimd", lambda gp: gp.tensor_mul(out=e3[:], in0=e2[:], in1=gain128[:]),
                 waits=[("dve", None)], incs=[("pool", 1)])
            P.op("gpsimd", lambda gp: gp.tensor_add(out=e3[:], in0=e3[:], in1=bias128[:]),
                 waits=[("pool", None)], incs=[("pool", 1)])
            P.op("scalar", lambda sc: sc.activation(out=e4[:], in_=e3[:], func=AF.Gelu),
                 waits=[("pool", None)], incs=[("act", 1)])
            for half in range(2):
                P.op("vector", lambda v, half=half: v.scalar_tensor_tensor(out=ob[:, half * 512:(half + 1) * 512], in0=e4[:, half * 512:(half + 1) * 512], scalar=mixsc[:, 2:3], in1=ps_l[half][:], op0=ALU.mult, op1=ALU.add),
                     waits=[("act", None), ("pe", None)], incs=[("dve", 1)])
            P.op("vector", lambda v, j=j: v.tensor_add(out=ob[:], in0=ob[:], in1=xb[j][:]),
                 waits=[("dve", None)], incs=[("dve", 1)])
            P.op("sync", lambda sy, s=s: sy.dma_start(out=out_packed[s * 128:(s + 1) * 128, :], in_=ob[:]),
                 waits=[("dve", None)], incs=[("out", 16)])
            P.op("tensor", lambda te: te.nop(), waits=[("dve", None)], incs=[])
        if debug >= 2:
            P.op("gpsimd", lambda gp: gp.dma_start(out=dbg["h_qT"][:, 0:128], in_=qTp[0][:]), waits=[("act", None), ("pe", None), ("dve", None)], incs=[("dpl", 16)])
            P.op("gpsimd", lambda gp: gp.dma_start(out=dbg["h_qT"][:, 128:256], in_=kTp[0][:]), incs=[("dpl", 16)])
            P.op("gpsimd", lambda gp: gp.dma_start(out=dbg["h_qT"][:, 256:768], in_=vtok[:]), incs=[("dpl", 16)])
            P.op("gpsimd", lambda gp: gp.dma_start(out=dbg["h_qT"][:, 768:896], in_=qTp[1][:]), incs=[("dpl", 16)])
            P.op("vector", lambda v: v.memset(sq5[:, 0:8], 0.0), waits=[("dve", None)], incs=[("dve", 1)])
            P.op("vector", lambda v: v.tensor_copy(out=sq5[:, 0:4], in_=mixsc[:]), waits=[("dve", None)], incs=[("dve", 1)])
            P.op("gpsimd", lambda gp: gp.dma_start(out=dbg["h_negb"], in_=sq5[:, 0:8]), waits=[("dve", None)], incs=[("dpl", 16)])
            P.op("gpsimd", lambda gp: gp.dma_start(out=dbg["h_avf"].rearrange("a b -> (a) (b)")[0:64, 0:128], in_=avb[0][:]), incs=[("dpl", 16)])
            P.op("gpsimd", lambda gp: gp.dma_start(out=dbg["h_nbuf"][0:64, :], in_=wol_bf[0:64, 0:1024]), incs=[("dpl", 16)])
            P.op("gpsimd", lambda gp: gp.dma_start(out=dbg["xn"][0:128, :], in_=gain128[:]), incs=[("dpl", 16)])
            P.op("gpsimd", lambda gp: gp.dma_start(out=dbg["xn"][128:256, :], in_=bias128[:]), incs=[("dpl", 16)])
            P.op("gpsimd", lambda gp: gp.dma_start(out=dbg["xn"][256:384, :], in_=e2[:]), incs=[("dpl", 16)])
            P.op("gpsimd", lambda gp: gp.dma_start(out=dbg["xn"][384:512, :], in_=e4[:]), incs=[("dpl", 16)])
            P.op("gpsimd", lambda gp: gp.dma_start(out=dbg["xn"][512:640, :], in_=e1[:]), incs=[("dpl", 16)])
            P.op("vector", lambda v: v.tensor_copy(out=e3[:, 0:512], in_=ps_l[0][:]), waits=[("dve", None)], incs=[("dve", 1)])
            P.op("vector", lambda v: v.tensor_copy(out=e3[:, 512:1024], in_=ps_l[1][:]), incs=[("dve", 1)])
            P.op("gpsimd", lambda gp: gp.dma_start(out=dbg["h_kvT"][:, 0:1024], in_=e3[:]), waits=[("dve", None)], incs=[("dpl", 16)])
        P.flush("p5")
        pctx.close()

    def p6():
        pctx = ExitStack()
        hb = [pctx.enter_context(nc.sbuf_tensor(f"p6hb{i}", [128, D], F32)) for i in range(2)]
        for tt in range(8):
            j = tt % 2
            P.op("sync", lambda sy, tt=tt, j=j: sy.dma_start(out=hb[j][:], in_=heavy_dram[tt * 128:(tt + 1) * 128, :]),
                 waits=[("out", None), ("dsc", None), ("dpl", None)], incs=[("dsy", 16)])
            P.op("gpsimd", lambda gp, tt=tt, j=j: gp.indirect_dma_start(out=out_packed,
                 out_offset=IndirectOffsetOnAxis(ownrow_sb[:, tt:tt + 1], 0), in_=hb[j][:], in_offset=None,
                 bounds_check=NS * 128 - 1, oob_is_err=False, compute_op=ALU.add),
                 waits=[("dsy", None)], incs=[("out", 16)])
        P.flush("p6")
        pctx.close()

    PH = globals().get("PHASES", 99)
    p0(); p0b(); p0c()
    if PH >= 1:
        p1()
    actx.close()
    if PH >= 2:
        p2()
    bctx.close()
    if PH >= 3:
        p3()
    if PH >= 4:
        p19_p15_p4(run15=True, run4=(PH >= 5))
    if PH >= 6:
        p5()
    if PH >= 7:
        p6()

    def fin(sy):
        return sy.nop()
    P.op("sync", fin, waits=[("out", None), ("dsy", None), ("dsc", None), ("dpl", None)])
    P.flush("fin")
    ctx.close()
    return nc, dbg


def host_inputs(core, xfull, w):
    b, par = core // 2, core % 2
    x_b = np.ascontiguousarray(xfull[b])
    slots = np.arange(NS)
    g = 2 * slots + par
    rowid = (g[None, :] * 128 + np.arange(128)[:, None]).astype(np.float32)
    clast = np.array([wslot(s) // 512 - 1 for s in range(NS)])
    bound = rowid - (clast[None, :] * 512).astype(np.float32)
    x_own = x_b.reshape(NT, 128, D)[g].reshape(NS * 128, D)
    t = np.arange(T)
    tile = t // 128
    own = (tile % 2) == par
    slot_of = (tile - par) // 2
    ownmap = np.where(own, slot_of * 128 + t % 128, 2**30).astype(np.uint32)
    return {
        "x_all": x_b, "x_own": np.ascontiguousarray(x_own),
        "rowid_f": rowid, "bound_f": bound.astype(np.float32),
        "rowid_u": rowid.astype(np.uint32), "ownmap": ownmap[:, None],
        "wqT": np.ascontiguousarray(w["L0_attn_W_q"].T), "wkT": np.ascontiguousarray(w["L0_attn_W_k"].T),
        "gain_in": w["L0_attn_gain"][None, :], "bias_in": w["L0_attn_bias"][None, :],
        "log_mix": w["L0_attn_log_mix"].reshape(1, 1), "log_scale": w["L0_attn_log_scale"].reshape(1, 1),
        "gam_l": w["L0_ffn_gamma_light"][None, :], "gam_h": w["L0_ffn_gamma_heavy"][None, :],
        "wq_l": w["L0_ffn_wq_light"], "wkv_l": w["L0_ffn_wkv_light"],
        "wo_l": np.ascontiguousarray(w["L0_ffn_wo_light"].reshape(8, 64, D)),
        "q_tok": w["L0_ffn_q_routing_token"][None, :], "kv_tok": w["L0_ffn_kv_routing_token"][None, :],
        "wq_h": w["L0_ffn_wq_heavy"],
        "wkv_hT": np.ascontiguousarray(w["L0_ffn_wkv_heavy"].T),
        "wo_h": np.ascontiguousarray(w["L0_ffn_wo_heavy"].reshape(8, 64, D)),
    }




def kernel(**inputs):
    xfull = np.ascontiguousarray(np.asarray(inputs["L0_attn_x"], dtype=np.float32))
    w = {k: np.asarray(v, dtype=np.float32) for k, v in inputs.items()}
    nc, _dbg = build(debug=0)
    in_maps = [host_inputs(c, xfull, w) for c in range(8)]
    res = run_bass_kernel_spmd(nc, in_maps, core_ids=list(range(8)))
    out = np.empty((4, 8192, 1024), np.float32)
    for c in range(8):
        b, par = c // 2, c % 2
        op = res.results[c]["out_packed"].reshape(NS, 128, 1024)
        out.reshape(4, NT, 128, 1024)[b, par::2] = op
    return out
